# version 01
# baseline (speedup 1.0000x reference)
import numpy as np
import ml_dtypes
import contextlib
from collections import defaultdict
import concourse.bass as bass
import concourse.mybir as mybir
from concourse.bass_utils import run_bass_kernel_spmd

F32 = mybir.dt.float32
BF16 = mybir.dt.bfloat16
I32 = mybir.dt.int32
AF = mybir.ActivationFunctionType
ALU = mybir.AluOpType
AX = mybir.AxisListType.X
bfdt = ml_dtypes.bfloat16

D = 1024
T = 2048
NKB = 8
EPS_LN = 1e-5
EPS_GN = 6.4e-4


class MK:
    def __init__(self):
        self.nc = bass.Bass("TRN2", target_bir_lowering=False)
        self.stack = contextlib.ExitStack()
        self.semh = {}
        self.ctr = defaultdict(int)
        self.waited = {}

    def sem(self, name):
        if name not in self.semh:
            self.semh[name] = self.stack.enter_context(self.nc.semaphore(name))
            self.ctr[name] = 0
        return self.semh[name]

    def inc(self, instr, name, k=1):
        h = self.sem(name)
        self.ctr[name] += k
        instr.then_inc(h, k)
        return self.ctr[name]

    def wait(self, eng, name, val=None):
        h = self.sem(name)
        if val is None:
            val = self.ctr[name]
        assert val <= self.ctr[name], f"wait target {val} > emitted {self.ctr[name]} for {name}"
        key = (id(eng), name)
        if self.waited.get(key, -1) >= val:
            return
        eng.wait_ge(h, val)
        self.waited[key] = val

    def join(self, *engs):
        for e_ in engs:
            for s_ in ("Tm", "Vc", "Sg", "Gc", "Wd", "Xd", "Ad", "Od", "Wc"):
                self.wait(e_, s_)

    def sb(self, name, shape, dtype):
        return self.stack.enter_context(self.nc.sbuf_tensor(name, shape, dtype))

    def ps(self, name, shape, dtype=F32):
        return self.stack.enter_context(self.nc.psum_tensor(name, shape, dtype))

    def din(self, name, shape, dtype=F32):
        return self.nc.dram_tensor(name, list(shape), dtype, kind="ExternalInput").ap()

    def dout(self, name, shape, dtype=F32):
        return self.nc.dram_tensor(name, list(shape), dtype, kind="ExternalOutput").ap()

    def dtmp(self, name, shape, dtype=F32):
        return self.nc.dram_tensor(name, list(shape), dtype, kind="Internal").ap()


def vsync(mk, ve, ins):
    v = mk.inc(ins, "Vc")
    mk.wait(ve, "Vc", v)


def rsqrt_newton(mk, ve, out, x, t2, iters=4):
    yi = out.bitcast(I32)
    ins = ve.tensor_single_scalar(yi, x.bitcast(I32), 1, ALU.arith_shift_right)
    vsync(mk, ve, ins)
    ins = ve.tensor_scalar(yi, yi, -1, 0x5F3759DF, ALU.mult, ALU.add)
    vsync(mk, ve, ins)
    for _ in range(iters):
        ins = ve.tensor_mul(t2, out, out)
        vsync(mk, ve, ins)
        ins = ve.tensor_mul(t2, t2, x)
        vsync(mk, ve, ins)
        ins = ve.tensor_scalar(t2, t2, -0.5, 1.5, ALU.mult, ALU.add)
        vsync(mk, ve, ins)
        last = ve.tensor_mul(out, out, t2)
        if _ < iters - 1:
            vsync(mk, ve, last)
    return last



def alloc_ln_scratch(mk, TC, pre="ln", pst=None, pbcb=None, r16s=None):
    if r16s is None:
        r16b = mk.sb(pre + "_r16", [128, NKB * TC], BF16)
        r16s = [r16b[:, i * TC:(i + 1) * TC] for i in range(NKB)]
    d = dict(
        TC=TC,
        r16s=r16s,
        ones16=mk.sb(pre + "_ones16", [128, 1], BF16),
        onesf=mk.sb(pre + "_onesf", [1, 128], F32),
        statrow=mk.sb(pre + "_statrow", [1, 5 * TC], F32),
        pst=pst if pst is not None else mk.ps(pre + "_pst", [1, 2 * TC]),
        pbcb=pbcb if pbcb is not None else mk.ps(pre + "_pbc", [128, 2 * TC]),
    )
    d["pbc"] = [d["pbcb"][:, 0:TC], d["pbcb"][:, TC:2 * TC]]
    sr = d["statrow"]
    for k, i in [("sm", 0), ("svar", 1), ("st1", 2), ("srs", 3), ("smrs", 4)]:
        d[k] = sr[:, i * TC:(i + 1) * TC]
    return d


def init_ln_scratch(mk, ln):
    ve = mk.nc.vector
    ve.memset(ln["ones16"][:], 1.0)
    ins = ve.memset(ln["onesf"][:], 1.0)
    mk.inc(ins, "Vc")


def build_ln(mk, ln, rbuf, xout, x16out=None, dbg=None):
    nc = mk.nc
    te, ve, sc = nc.tensor, nc.vector, nc.scalar
    TC = ln["TC"]
    tm_entry = mk.ctr["Tm"]
    mk.wait(ve, "Tm", tm_entry)
    for i in range(NKB):
        vins = ve.tensor_copy(ln["r16s"][i], rbuf[i])
    vc_all = mk.inc(vins, "Vc")
    mk.wait(te, "Vc", vc_all)
    for i in range(NKB):
        ins = te.matmul(ln["pst"][0:1, 0:TC], ln["ones16"][:], ln["r16s"][i],
                        start=(i == 0), stop=(i == NKB - 1))
    tm_a = mk.inc(ins, "Tm")
    mk.wait(sc, "Tm", tm_a)
    for i in range(NKB):
        sins = sc.activation(ln["r16s"][i], rbuf[i], AF.Square)
    sg_all = mk.inc(sins, "Sg")
    mk.wait(te, "Sg", sg_all)
    for i in range(NKB):
        ins = te.matmul(ln["pst"][0:1, TC:2 * TC], ln["ones16"][:], ln["r16s"][i],
                        start=(i == 0), stop=(i == NKB - 1))
    tm_st = mk.inc(ins, "Tm")
    mk.wait(ve, "Tm", tm_st)
    ins = ve.tensor_scalar_mul(ln["sm"], ln["pst"][0:1, 0:TC], 1.0 / D)
    ve.tensor_scalar_mul(ln["svar"], ln["pst"][0:1, TC:2 * TC], 1.0 / D)
    vsync(mk, ve, ins)
    ins = ve.tensor_mul(ln["st1"], ln["sm"], ln["sm"])
    vsync(mk, ve, ins)
    ins = ve.tensor_scalar(ln["svar"], ln["svar"], 1.0, EPS_LN, ALU.mult, ALU.add)
    vsync(mk, ve, ins)
    ins = ve.tensor_sub(ln["svar"], ln["svar"], ln["st1"])
    vsync(mk, ve, ins)
    rsqrt_newton(mk, ve, ln["srs"], ln["svar"], ln["st1"])
    ins = ve.tensor_mul(ln["smrs"], ln["sm"], ln["srs"])
    vc_rs = mk.inc(ins, "Vc")
    mk.wait(ve, "Vc", vc_rs)
    mk.wait(te, "Vc", vc_rs)
    te.matmul(ln["pbc"][0], ln["onesf"][:], ln["srs"], start=True, stop=True)
    ins = te.matmul(ln["pbc"][1], ln["onesf"][:], ln["smrs"], start=True, stop=True)
    tm_bc = mk.inc(ins, "Tm")
    if dbg is not None:
        gp = nc.gpsimd
        mk.wait(gp, "Vc", vc_rs)
        ins = gp.dma_start(dbg["stat"], ln["statrow"][:])
        mk.inc(ins, "Od", 16)
    mk.wait(ve, "Tm", tm_bc)
    for i in range(NKB):
        ve.tensor_mul(xout[i], rbuf[i], ln["pbc"][0])
        ins = ve.tensor_sub(xout[i], xout[i], ln["pbc"][1])
        if x16out is not None:
            ins = ve.tensor_copy(x16out[i], xout[i])
    return mk.inc(ins, "Vc")



def build_ffn_phase(mk, xh_dram, x2_dram, wd, dbg=None):
    nc = mk.nc
    sy, te, ve, sc, gp = nc.sync, nc.tensor, nc.vector, nc.scalar, nc.gpsimd
    TC = 256
    NTC = T // TC
    HB = 32

    x16 = [mk.sb(f"ffn_x16_{i}", [128, T], BF16) for i in range(NKB)]
    w1b = [mk.sb(f"ffn_w1_{i}", [128, 4096], BF16) for i in range(NKB)]
    w2b = [mk.sb(f"ffn_w2_{i}", [128, 1024], BF16) for i in range(HB)]
    wst = [mk.sb("ffn_wst_0", [128, 1024], F32)]
    h16 = [mk.sb(f"ffn_h16_{i}", [128, TC], BF16) for i in range(HB)]
    b1c = mk.sb("ffn_b1c", [128, 32], F32)
    b2c = mk.sb("ffn_b2c", [128, 8], F32)
    rbuf = [mk.sb(f"ffn_r_{i}", [128, TC], F32) for i in range(NKB)]
    xhres = [mk.sb(f"ffn_xh_{i}", [128, TC], F32) for i in range(NKB)]
    ln = alloc_ln_scratch(mk, TC)

    pout = [mk.ps(f"ffn_pout_{i}", [128, TC])[:, :] for i in range(4)]
    ph = [mk.ps(f"ffn_ph_{i}", [128, TC])[:, :] for i in range(2)]

    ins = sy.dma_start(b1c[:], wd["b1c"]); mk.inc(ins, "Wd", 16)
    ins = sy.dma_start(b2c[:], wd["b2c"]); mk.inc(ins, "Wd", 16)
    init_ln_scratch(mk, ln)

    loads = []
    for i in range(NKB):
        for q in range(2):
            loads.append((xh_dram[i * 128:(i + 1) * 128, q * 1024:(q + 1) * 1024],
                          x16[i][:, q * 1024:(q + 1) * 1024], 1024))
    for i in range(NKB):
        for q in range(4):
            loads.append((wd["w1"][i * 128:(i + 1) * 128, q * 1024:(q + 1) * 1024],
                          w1b[i][:, q * 1024:(q + 1) * 1024], 1024))
    for i in range(HB):
        loads.append((wd["w2"][i * 128:(i + 1) * 128, :], w2b[i][:], 1024))
    cast_done = []
    for li, (dsl, dst, ncols) in enumerate(loads):
        st = wst[0]
        if li >= 1:
            mk.wait(sy, "Wc", cast_done[li - 1])
        ins = sy.dma_start(st[:, 0:ncols], dsl)
        wd_val = mk.inc(ins, "Wd", 16)
        eng = ve if li % 2 == 0 else sc
        mk.wait(eng, "Wd", wd_val)
        if eng is ve:
            ins = eng.tensor_copy(dst, st[:, 0:ncols])
        else:
            ins = eng.copy(dst, st[:, 0:ncols])
        cast_done.append(mk.inc(ins, "Wc"))
    wc_all = mk.ctr["Wc"]

    sg_h = [0, 0]
    tm_h16free = [0, 0]
    vc_res = [0]
    sg_pout = 0
    vc_apply = 0

    for tc in range(NTC):
        t0 = tc * TC
        tsl = slice(t0, t0 + TC)
        if tc >= 1:
            mk.wait(gp, "Vc", vc_res[0])
        for i in range(NKB):
            ins = gp.dma_start(xhres[i][:], xh_dram[i * 128:(i + 1) * 128, tsl])
            xd_res = mk.inc(ins, "Xd", 16)

        mk.wait(te, "Wc", wc_all)
        for hb in range(HB):
            j = hb % 2
            hp = ph[j]
            if sg_h[j]:
                mk.wait(te, "Sg", sg_h[j])
            for kb in range(NKB):
                ins = te.matmul(hp, w1b[kb][:, hb * 128:(hb + 1) * 128],
                                x16[kb][:, tsl], start=(kb == 0), stop=(kb == NKB - 1))
            tm_hmm = mk.inc(ins, "Tm")
            mk.wait(sc, "Tm", tm_hmm)
            ins = sc.activation(h16[hb][:], hp, AF.Gelu,
                                bias=b1c[:, hb:hb + 1], scale=1.0)
            sg_h[j] = mk.inc(ins, "Sg")
            if dbg is not None and tc == 0 and hb == 0:
                mk.wait(gp, "Sg", sg_h[j])
                ins = gp.dma_start(dbg["h16"], h16[hb][:])
                mk.inc(ins, "Od", 16)
        sg_gelus = mk.ctr["Sg"]
        mk.wait(te, "Sg", sg_gelus)
        if sg_pout:
            mk.wait(te, "Sg", sg_pout)
        if vc_apply:
            mk.wait(sc, "Od", mk.ctr["Od"])
        for half in range(2):
            for hb in range(HB):
                for m4 in range(4):
                    mb = half * 4 + m4
                    ins = te.matmul(pout[m4], w2b[hb][:, mb * 128:(mb + 1) * 128],
                                    h16[hb][:], start=(hb == 0), stop=(hb == HB - 1))
            tm_out2 = mk.inc(ins, "Tm")
            mk.wait(sc, "Tm", tm_out2)
            for m4 in range(4):
                mb = half * 4 + m4
                ins = sc.activation(rbuf[mb][:], pout[m4], AF.Identity, bias=b2c[:, mb:mb + 1])
            sg_pout = mk.inc(ins, "Sg")
            if half == 0:
                mk.wait(te, "Sg", sg_pout)
        mk.wait(ve, "Sg", sg_pout)
        mk.wait(ve, "Xd", xd_res)
        for mb in range(NKB):
            ins = ve.tensor_add(rbuf[mb][:], rbuf[mb][:], xhres[mb][:])
        vc_res[0] = mk.inc(ins, "Vc")
        if dbg is not None and tc == 0:
            mk.wait(gp, "Vc", vc_res[0])
            for mb in range(NKB):
                ins = gp.dma_start(dbg["r"][mb * 128:(mb + 1) * 128, :], rbuf[mb][:])
                mk.inc(ins, "Od", 16)

        vc_apply = build_ln(mk, ln, [r[:] for r in rbuf], [r[:] for r in rbuf],
                            dbg=(dbg if tc == 0 else None))

        mk.wait(gp, "Vc", vc_apply)
        for i in range(NKB):
            ins = gp.dma_start(x2_dram[i * 128:(i + 1) * 128, tsl], rbuf[i][:])
            mk.inc(ins, "Od", 16)

    mk.wait(sy, "Od", mk.ctr["Od"])



def host_col(v, nb=None):
    v = np.asarray(v, np.float32)
    n = v.shape[0]
    return np.ascontiguousarray(v.reshape(n // 128, 128).T)




def build_attn_phase(mk, x2_dram, x3_dram, wd, dbg=None):
    nc = mk.nc
    sy, te, ve, sc, gp = nc.sync, nc.tensor, nc.vector, nc.scalar, nc.gpsimd
    NH = 16

    qT = [mk.sb(f"at_q_{i}", [128, T], BF16) for i in range(NKB)]
    kT = [mk.sb(f"at_k_{i}", [128, T], BF16) for i in range(NKB)]
    V16 = [mk.sb(f"at_v_{i}", [128, 1040], BF16) for i in range(16)]
    wXb = [mk.sb(f"at_w_{i}", [128, D], BF16) for i in range(NKB)]
    wst = mk.sb("at_wst", [128, 2048], F32)
    ropeC = mk.sb("at_ropeC", [128, T], BF16)
    ropeS = mk.sb("at_ropeS", [128, T], BF16)
    pswap = mk.sb("at_pswap", [128, 128], BF16)
    hsel = mk.sb("at_hsel", [128, 16 * 8], BF16)
    dmask = mk.sb("at_dmask", [128, 512 * 4], BF16)
    pre16 = mk.sb("at_pre16", [128, 512], BF16)
    sw16 = mk.sb("at_sw16", [128, 512], BF16)
    t116 = mk.sb("at_t116", [128, 512], BF16)
    coldr = mk.dtmp("at_coldr", [16], F32)
    xstack = contextlib.ExitStack()
    x16 = [xstack.enter_context(mk.nc.sbuf_tensor(f"at_x16_{i}", [128, T], BF16))
           for i in range(NKB)]

    P0 = mk.ps("at_P0", [128, 512])[:, :]
    P1 = mk.ps("at_P1", [128, 512])[:, :]
    P2 = mk.ps("at_P2", [128, 512])[:, :]
    P3 = mk.ps("at_P3", [128, 512])[:, :]
    pv = mk.ps("at_pv", [65, 512])[:, :]

    for name, dst in [("ropeC", ropeC), ("ropeS", ropeS), ("pswap", pswap),
                      ("hsel", hsel), ("dmask", dmask)]:
        ins = sy.dma_start(dst[:], wd[name])
        mk.inc(ins, "Wd", 16)
    for tb in range(16):
        ap = V16[tb][:].rearrange("p (h c) -> p h c", c=65)[:, :, 64:65]
        ins = ve.memset(ap, 1.0)
    vc_init = mk.inc(ins, "Vc")

    for i in range(NKB):
        ins = gp.dma_start(wst[:, 0:T], x2_dram[i * 128:(i + 1) * 128, :])
        xd = mk.inc(ins, "Xd", 16)
        mk.wait(ve, "Xd", xd)
        ins = ve.tensor_copy(x16[i][:], wst[:, 0:T])
        vc_x = mk.inc(ins, "Vc")
        mk.wait(gp, "Vc", vc_x)
    vc_x16 = mk.ctr["Vc"]

    def load_w(wdram):
        mk.wait(sy, "Tm", mk.ctr["Tm"])
        mk.wait(sy, "Vc", mk.ctr["Vc"])
        mk.wait(sy, "Sg", mk.ctr["Sg"])
        for i in range(NKB):
            ins = sy.dma_start(wst[:, 0:1024], wdram[i * 128:(i + 1) * 128, :])
            wv_ = mk.inc(ins, "Wd", 16)
            eng = sc if i % 2 == 0 else ve
            mk.wait(eng, "Tm", mk.ctr["Tm"])
            mk.wait(eng, "Wd", wv_)
            if eng is sc:
                ins = eng.copy(wXb[i][:], wst[:, 0:1024])
                mk.inc(ins, "Sg")
                mk.wait(sy, "Sg", mk.ctr["Sg"])
            else:
                ins = eng.tensor_copy(wXb[i][:], wst[:, 0:1024])
                mk.inc(ins, "Vc")
                mk.wait(sy, "Vc", mk.ctr["Vc"])
        return mk.ctr["Sg"], mk.ctr["Vc"]

    for wname, dstT in [("wq", qT), ("wk", kT)]:
        sgw, vcw = load_w(wd[wname])
        mk.wait(te, "Sg", sgw)
        mk.wait(te, "Vc", max(vcw, vc_x16))
        for mb in range(NKB):
            for tb in range(4):
                ts4 = slice(tb * 512, (tb + 1) * 512)
                pj = P0 if (mb * 4 + tb) % 2 == 0 else P1
                mk.wait(te, "Sg", mk.ctr["Sg"])
                mk.wait(te, "Vc", mk.ctr["Vc"])
                mk.wait(te, "Gc", mk.ctr.get("Gc", 0))
                for kb in range(NKB):
                    ins = te.matmul(pj, wXb[kb][:, mb * 128:(mb + 1) * 128],
                                    x16[kb][:, ts4], start=(kb == 0), stop=(kb == NKB - 1))
                tmv = mk.inc(ins, "Tm")
                mk.wait(sc, "Tm", tmv)
                mk.wait(sc, "Vc", mk.ctr["Vc"])
                ins = sc.copy(pre16[:], pj)
                sgp = mk.inc(ins, "Sg")
                mk.wait(te, "Sg", sgp)
                ins = te.matmul(P2, pswap[:], pre16[:], start=True, stop=True)
                tms = mk.inc(ins, "Tm")
                mk.wait(sc, "Tm", tms)
                mk.wait(sc, "Gc", mk.ctr.get("Gc", 0))
                ins = sc.copy(sw16[:], P2)
                sgs = mk.inc(ins, "Sg")
                mk.wait(ve, "Sg", sgp)
                ins = ve.tensor_mul(t116[:], pre16[:], ropeC[:, ts4])
                mk.inc(ins, "Vc")
                mk.wait(gp, "Sg", sgs)
                ins = gp.tensor_mul(sw16[:], sw16[:], ropeS[:, ts4])
                gc1 = mk.inc(ins, "Gc")
                mk.wait(ve, "Gc", gc1)
                ins = ve.tensor_add(dstT[mb][:, ts4], t116[:], sw16[:])
                mk.inc(ins, "Vc")

    sgw, vcw = load_w(wd["wv"])
    mk.wait(te, "Sg", sgw)
    mk.wait(te, "Vc", max(vcw, vc_init))
    for tb in range(16):
        for half in range(2):
            pj = P0 if (tb * 2 + half) % 2 == 0 else P1
            mk.wait(te, "Vc", mk.ctr["Vc"])
            for kb in range(NKB):
                ins = te.matmul(pj, x16[kb][:, tb * 128:(tb + 1) * 128],
                                wXb[kb][:, half * 512:(half + 1) * 512],
                                start=(kb == 0), stop=(kb == NKB - 1))
            tmv = mk.inc(ins, "Tm")
            mk.wait(ve, "Tm", tmv)
            src = pj.rearrange("p (h c) -> p h c", c=64)
            dst = V16[tb][:, half * 520:(half + 1) * 520].rearrange("p (h c) -> p h c", c=65)[:, :, 0:64]
            ins = ve.tensor_copy(dst, src)
            mk.inc(ins, "Vc")

    if dbg is not None:
        mk.wait(gp, "Vc", mk.ctr["Vc"])
        ins = gp.dma_start(dbg["q"], qT[0][:])
        mk.inc(ins, "Od", 16)
        ins = gp.dma_start(dbg["k"], kT[0][:])
        mk.inc(ins, "Od", 16)
        ins = gp.dma_start(dbg["v"], V16[0][:])
        mk.inc(ins, "Od", 16)
    xstack.close()
    sq16 = mk.sb("at_sq16", [128, 4096], BF16)
    PT = [mk.sb(f"at_pt_{i}", [128, 512], BF16) for i in range(16)]
    o16 = mk.sb("at_o16", [128, 512], BF16)
    attnT = [mk.sb(f"at_attnT_{i}", [128, 512], BF16) for i in range(NKB)]
    rrow = mk.sb("at_rrow", [128, 512], F32)
    nstat = mk.sb("at_nstat", [128, 24], F32)
    onesall = mk.sb("at_onesall", [128, 128], F32)
    rowtmp = mk.sb("at_rowtmp", [1, 16], F32)
    rbuf = [mk.sb(f"at_r_{i}", [128, 256], F32) for i in range(NKB)]
    xres = [mk.sb(f"at_xh_{i}", [128, 256], F32) for i in range(NKB)]
    ln = alloc_ln_scratch(mk, 256, "atln")
    init_ln_scratch(mk, ln)
    ins = ve.memset(onesall[:], 1.0)
    mk.inc(ins, "Vc")
    for idx, srcT in [(0, qT), (1, kT)]:
        for tb in range(4):
            ts4 = slice(tb * 512, (tb + 1) * 512)
            mk.wait(sc, "Tm", mk.ctr["Tm"])
            for i in range(NKB):
                ins = sc.activation(sq16[:, i * 512:(i + 1) * 512], srcT[i][:, ts4], AF.Square)
            sgq = mk.inc(ins, "Sg")
            mk.wait(te, "Sg", sgq)
            for i in range(NKB):
                ins = te.matmul(P3[0:16, :], hsel[:, i * 16:(i + 1) * 16],
                                sq16[:, i * 512:(i + 1) * 512],
                                start=(i == 0), stop=(i == NKB - 1))
            mk.inc(ins, "Tm")
            mk.wait(ve, "Tm", mk.ctr["Tm"])
            if idx == 0 and tb == 0:
                ins = ve.reduce_max(nstat[0:16, 0:1], P3[0:16, :], axis=AX)
            elif idx == 1 and tb == 0:
                ins = ve.reduce_max(nstat[0:16, 1:2], P3[0:16, :], axis=AX)
            else:
                ve.reduce_max(nstat[0:16, 4:5], P3[0:16, :], axis=AX)
                ins = ve.tensor_max(nstat[0:16, idx:idx + 1], nstat[0:16, idx:idx + 1],
                                    nstat[0:16, 4:5])
            vcq = mk.inc(ins, "Vc")
            mk.wait(te, "Vc", vcq)
    ve.tensor_mul(nstat[0:16, 2:3], nstat[0:16, 0:1], nstat[0:16, 1:2])
    ve.tensor_scalar_add(nstat[0:16, 2:3], nstat[0:16, 2:3], 1e-12)
    rsqrt_newton(mk, ve, nstat[0:16, 3:4], nstat[0:16, 2:3], nstat[0:16, 4:5])
    ve.tensor_mul(nstat[0:16, 5:6], nstat[0:16, 2:3], nstat[0:16, 3:4])
    ins = ve.tensor_scalar(nstat[0:16, 5:6], nstat[0:16, 5:6], -0.125, 70.0,
                           ALU.mult, ALU.add)
    vcm = mk.inc(ins, "Vc")
    mk.wait(gp, "Vc", vcm)
    ins = gp.dma_start(coldr.rearrange("(a b) -> a b", b=1), nstat[0:16, 5:6])
    gd = mk.inc(ins, "Gd", 16)
    mk.wait(gp, "Gd", gd)
    ins = gp.dma_start(rowtmp[:], coldr.rearrange("(a b) -> a b", a=1))
    gd = mk.inc(ins, "Gd", 16)
    mk.wait(te, "Gd", gd)
    mk.wait(te, "Vc", mk.ctr["Vc"])
    ins = te.matmul(P2[:, 0:16], ln["onesf"][:], rowtmp[:], start=True, stop=True)
    tmb = mk.inc(ins, "Tm")
    mk.wait(ve, "Tm", tmb)
    ins = ve.tensor_copy(nstat[:, 6:22], P2[:, 0:16])
    vc_mh = mk.inc(ins, "Vc")

    if dbg is not None:
        mk.wait(gp, "Vc", vc_mh)
        ins = gp.dma_start(dbg["nstat"], nstat[:])
        mk.inc(ins, "Od", 16)
    sg_wo, vc_wo = load_w(wd["wo"])

    sg_p = {0: 0, 1: 0}
    for qb in range(4):
        nkb = (qb + 1) * 4
        for h in range(NH):
            mt = h // 2
            hr = slice(64 * (h % 2), 64 * (h % 2) + 64)
            for kb in range(nkb):
                j = kb % 2
                pj = P0 if j == 0 else P1
                if sg_p[j]:
                    mk.wait(te, "Sg", sg_p[j])
                ins = te.matmul(pj, kT[mt][hr, kb * 128:(kb + 1) * 128],
                                qT[mt][hr, qb * 512:(qb + 1) * 512], start=True, stop=True)
                tms = mk.inc(ins, "Tm")
                mk.wait(sc, "Tm", tms)
                mk.wait(sc, "Vc", vc_mh)
                mk.wait(sc, "Gc", mk.ctr.get("Gc", 0))
                ins = sc.activation(PT[kb][:], pj, AF.Exp,
                                    bias=nstat[:, 6 + h:7 + h], scale=0.125)
                sg_p[j] = mk.inc(ins, "Sg")
            sg_exp = mk.ctr["Sg"]
            mk.wait(gp, "Sg", sg_exp)
            for dkb in range(4):
                kb = qb * 4 + dkb
                ins = gp.tensor_mul(PT[kb][:], PT[kb][:], dmask[:, dkb * 512:(dkb + 1) * 512])
            gc_mask = mk.inc(ins, "Gc")
            mk.wait(te, "Sg", sg_exp)
            mk.wait(te, "Gc", gc_mask)
            mk.wait(te, "Vc", mk.ctr["Vc"])
            mk.wait(te, "Sg", mk.ctr["Sg"])
            for kb in range(nkb):
                ins = te.matmul(pv, V16[kb][:, 65 * h:65 * h + 65], PT[kb][:],
                                start=(kb == 0), stop=(kb == nkb - 1))
            tm_pv = mk.inc(ins, "Tm")
            mk.wait(sc, "Tm", tm_pv)
            mk.wait(sc, "Gd", mk.ctr.get("Gd", 0))
            mk.wait(sc, "Vc", mk.ctr["Vc"])
            ins = sc.copy(o16[0:64, :], pv[0:64, :])
            sg_o = mk.inc(ins, "Sg")
            mk.wait(ve, "Tm", tm_pv)
            ins = ve.reciprocal(rrow[64:65, :], pv[64:65, :])
            vc_rr = mk.inc(ins, "Vc")
            mk.wait(te, "Vc", vc_rr)
            ins = te.matmul(P2[0:64, :], onesall[64:65, 0:64], rrow[64:65, :],
                            start=True, stop=True)
            tm_nb = mk.inc(ins, "Tm")
            mk.wait(ve, "Tm", tm_nb)
            mk.wait(ve, "Sg", sg_o)
            if h % 2 == 0:
                ins = ve.tensor_tensor(attnT[mt][0:64, :], o16[0:64, :], P2[0:64, :], ALU.mult)
                mk.inc(ins, "Vc")
            else:
                ins = ve.tensor_tensor(o16[0:64, :], o16[0:64, :], P2[0:64, :], ALU.mult)
                vc_n = mk.inc(ins, "Vc")
                mk.wait(gp, "Vc", vc_n)
                ins = gp.dma_start(attnT[mt][64:128, 0:512], o16[0:64, :])
                mk.inc(ins, "Gd", 16)
        if dbg is not None and qb == 0:
            mk.wait(gp, "Vc", mk.ctr["Vc"])
            mk.wait(gp, "Gd", mk.ctr["Gd"])
            ins = gp.dma_start(dbg["attn"], attnT[0][:])
            mk.inc(ins, "Od", 16)
            ins = gp.dma_start(dbg["pt"], PT[0][:])
            mk.inc(ins, "Od", 16)
        mk.wait(te, "Sg", sg_wo)
        mk.wait(te, "Vc", mk.ctr["Vc"])
        mk.wait(te, "Gd", mk.ctr["Gd"])
        for half in range(2):
            hs = slice(qb * 512 + half * 256, qb * 512 + half * 256 + 256)
            mk.wait(gp, "Vc", mk.ctr["Vc"])
            for i in range(NKB):
                ins = gp.dma_start(xres[i][:], x2_dram[i * 128:(i + 1) * 128, hs])
                xd_r = mk.inc(ins, "Xd", 16)
            for mb in range(NKB):
                pj = (P0 if mb % 2 == 0 else P1)[:, 0:256]
                mk.wait(te, "Sg", mk.ctr["Sg"])
                for kb in range(NKB):
                    ins = te.matmul(pj, wXb[kb][:, mb * 128:(mb + 1) * 128],
                                    attnT[kb][:, half * 256:half * 256 + 256],
                                    start=(kb == 0), stop=(kb == NKB - 1))
                tmo = mk.inc(ins, "Tm")
                mk.wait(sc, "Tm", tmo)
                mk.wait(sc, "Od", mk.ctr.get("Od", 0))
                ins = sc.copy(rbuf[mb][:], pj)
                mk.inc(ins, "Sg")
            sg_wo_e = mk.ctr["Sg"]
            mk.wait(ve, "Sg", sg_wo_e)
            mk.wait(ve, "Xd", xd_r)
            for mb in range(NKB):
                ins = ve.tensor_add(rbuf[mb][:], rbuf[mb][:], xres[mb][:])
            mk.inc(ins, "Vc")
            vc_apply = build_ln(mk, ln, [r[:] for r in rbuf], [r[:] for r in rbuf])
            mk.wait(gp, "Vc", vc_apply)
            for i in range(NKB):
                ins = gp.dma_start(x3_dram[i * 128:(i + 1) * 128, hs], rbuf[i][:])
                mk.inc(ins, "Od", 16)
    mk.wait(sy, "Od", mk.ctr["Od"])


def host_attn_tables():
    pos = np.arange(T, dtype=np.float64)
    inv = 10000.0 ** (-np.arange(0, 64, 2, dtype=np.float64) / 64)
    ang = pos[None, :] * inv[:, None]
    c = np.cos(ang); s = np.sin(ang)
    C = np.concatenate([c, c, c, c], 0).astype(bfdt)
    S = np.concatenate([-s, s, -s, s], 0).astype(bfdt)
    P = np.zeros((128, 128), np.float32)
    for j in range(128):
        src = j + 32 if (j % 64) < 32 else j - 32
        P[src, j] = 1.0
    hsel = np.zeros((8, 128, 16), np.float32)
    for m in range(8):
        hsel[m, 0:64, 2 * m] = 1.0
        hsel[m, 64:128, 2 * m + 1] = 1.0
    hs = np.concatenate([hsel[m] for m in range(8)], axis=1)
    dm = np.zeros((4, 128, 512), np.float32)
    for dkb in range(4):
        for ss in range(128):
            dm[dkb, ss, dkb * 128 + ss:] = 1.0
    dmc = np.concatenate([dm[i] for i in range(4)], axis=1)
    return dict(ropeC=np.ascontiguousarray(C), ropeS=np.ascontiguousarray(S),
                pswap=P.astype(bfdt), hsel=hs.astype(bfdt), dmask=dmc.astype(bfdt))


def host_attn_weights(W_qkv, W_o):
    perm = np.zeros(64, np.int64)
    perm[0:32] = np.arange(0, 64, 2)
    perm[32:64] = np.arange(1, 64, 2)
    full_perm = np.concatenate([h * 64 + perm for h in range(16)])
    wq = np.ascontiguousarray(W_qkv[0:1024][full_perm].T)
    wk = np.ascontiguousarray(W_qkv[1024:2048][full_perm].T)
    wv = np.ascontiguousarray(W_qkv[2048:3072].T)
    wo = np.ascontiguousarray(W_o.T)
    return dict(wq=wq, wk=wk, wv=wv, wo=wo)



def build_moe_phase(mk, x3_dram, out_dram, wd, dbg=None):
    nc = mk.nc
    sy, te, ve, sc, gp = nc.sync, nc.tensor, nc.vector, nc.scalar, nc.gpsimd
    NE = 8
    AL = 1.702

    x16 = [mk.sb(f"mo_x16_{i}", [128, T], BF16) for i in range(NKB)]
    wg16 = [mk.sb(f"mo_wg_{i}", [128, D], BF16) for i in range(NKB)]
    wu16 = [mk.sb(f"mo_wu_{i}", [128, D], BF16) for i in range(NKB)]
    wdn8 = [mk.sb(f"mo_wdn_{i}", [128, 512], BF16) for i in range(NKB)]
    acst8 = [mk.sb(f"mo_acst_{i}", [128, 512], BF16) for i in range(NKB)]
    wst = mk.sb("mo_wst", [128, 2048], F32)
    rwb = mk.sb("mo_rwb", [128, 64], BF16)
    rwf = mk.sb("mo_rwf", [128, 64], F32)
    xf32 = [mk.sb(f"mo_xf32_{i}", [128, 128], F32) for i in range(NKB)]
    gst = mk.sb("mo_gst", [128, 512], F32)
    slst = mk.sb("mo_slst", [128, 512], F32)
    ust = mk.sb("mo_ust", [128, 512], F32)
    u2st = mk.sb("mo_u2st", [128, 512], F32)
    m1st = mk.sb("mo_m1st", [128, 512], F32)
    act16 = mk.sb("mo_act16", [128, 512], BF16)
    bgc = mk.sb("mo_bgc", [128, 64], F32)
    buc = mk.sb("mo_buc", [128, 64], F32)
    wT = mk.sb("mo_wT", [8, T], F32)
    lgall = mk.sb("mo_lgall", [128, 128], F32)
    e1a = mk.sb("mo_e1a", [128, 128], F32)
    e2a = mk.sb("mo_e2a", [128, 128], F32)
    mka = mk.sb("mo_mka", [128, 128], F32)
    wca = mk.sb("mo_wca", [128, 128], F32)
    rstat = mk.sb("mo_rstat", [128, 80], F32)
    wT16 = mk.sb("mo_wT16", [8, T], BF16)
    rst = mk.sb("mo_rst", [128, 10], F32)
    rmsk = mk.sb("mo_rmsk", [128, 8], F32)
    rlog = mk.sb("mo_rlog", [128, 8], F32)
    re1 = mk.sb("mo_re1", [128, 8], F32)
    onesf_s = mk.sb("mo_onesf_s", [1, 128], F32)
    idf = mk.sb("mo_idf", [128, 128], F32)
    dBf = mk.sb("mo_dBf", [8, 1024], F32)
    dB = mk.sb("mo_dB", [8, 1024], BF16)
    rbuf = [mk.sb(f"mo_r_{i}", [128, 512], F32) for i in range(NKB)]
    xres = [mk.sb(f"mo_xh_{i}", [128, 512], F32) for i in range(NKB)]
    ln = alloc_ln_scratch(mk, 256, "moln")

    pG = mk.ps("mo_pG", [128, 512])[:, :]
    pU = mk.ps("mo_pU", [128, 512])[:, :]
    pdown = [mk.ps(f"mo_pd_{i}", [128, 512])[:, :] for i in range(4)]

    wrow = mk.sb("mo_wrow", [1, T], F32)
    if dbg is not None:
        act_dr = mk.nc.dram_tensor("act_dr", [NE, D, T], BF16, kind="ExternalOutput").ap()
    else:
        act_dr = mk.dtmp("mo_act_dr", [NE, D, T], BF16)
    down_dr = mk.dtmp("mo_down_dr", [NE, D, D], BF16)
    wT_dr = (mk.nc.dram_tensor("wT_dr", [8, T], F32, kind="ExternalOutput").ap()
             if dbg is not None else mk.dtmp("mo_wT_dr", [8, T], F32))
    r_dr = mk.nc.dram_tensor("r_dr", [D, T], F32,
                             kind="ExternalOutput" if dbg is not None else "Internal").ap()

    init_ln_scratch(mk, ln)
    ve.memset(onesf_s[:], 1.0 / AL)
    ins = sy.dma_start(idf[:], wd["idf"]); mk.inc(ins, "Wd", 16)
    ins = sy.dma_start(bgc[:], wd["bgc"]); mk.inc(ins, "Wd", 16)
    ins = sy.dma_start(buc[:], wd["buc"]); mk.inc(ins, "Wd", 16)
    ins = sy.dma_start(dBf[:], wd["dbias"]); wv_ = mk.inc(ins, "Wd", 16)
    mk.wait(ve, "Wd", wv_)
    ins = ve.tensor_copy(dB[:], dBf[:])
    vc_db = mk.inc(ins, "Vc")

    for i in range(NKB):
        ins = sy.dma_start(wst[:, 0:T], x3_dram[i * 128:(i + 1) * 128, :])
        xd = mk.inc(ins, "Xd", 16)
        mk.wait(ve, "Xd", xd)
        ins = ve.tensor_copy(x16[i][:], wst[:, 0:T])
        vc_x = mk.inc(ins, "Vc")
        mk.wait(sy, "Vc", vc_x)
    vc_x16 = mk.ctr["Vc"]

    for kb in range(NKB):
        ins = sy.dma_start(rwf[:, kb * 8:(kb + 1) * 8], wd["router"][kb * 128:(kb + 1) * 128, :])
        mk.inc(ins, "Wd", 16)
    vc_rw = mk.ctr["Vc"]

    mk.wait(te, "Vc", vc_rw)
    mk.wait(te, "Wd", mk.ctr["Wd"])
    mk.wait(ve, "Wd", mk.ctr["Wd"])
    mk.wait(sy, "Wd", mk.ctr["Wd"])
    tm_prev = 0
    for tb in range(16):
        pj = pG[:, 0:8]
        mk.wait(te, "Vc", mk.ctr["Vc"])
        if tm_prev:
            mk.wait(sy, "Tm", tm_prev)
        for kb in range(NKB):
            ins = sy.dma_start(xf32[kb][:], x3_dram[kb * 128:(kb + 1) * 128,
                                                    tb * 128:(tb + 1) * 128])
            xd_ = mk.inc(ins, "Xd", 16)
        mk.wait(te, "Xd", xd_)
        for kb in range(NKB):
            ins = te.matmul(pj, xf32[kb][:], rwf[:, kb * 8:(kb + 1) * 8],
                            start=(kb == 0), stop=(kb == NKB - 1))
        tm_prev = tml = mk.inc(ins, "Tm")
        mk.wait(ve, "Tm", tml)
        ins = ve.tensor_copy(lgall[:, tb * 8:(tb + 1) * 8], pj)
        mk.inc(ins, "Vc")
    lg3 = lgall[:].rearrange("p (t e) -> p t e", e=8)
    e13 = e1a[:].rearrange("p (t e) -> p t e", e=8)
    e23 = e2a[:].rearrange("p (t e) -> p t e", e=8)
    mk3 = mka[:].rearrange("p (t e) -> p t e", e=8)

    def bc16(ap2d):
        return ap2d.rearrange("p (t o) -> p t o", o=1).broadcast_to([128, 16, 8])

    ins = ve.reduce_max(rstat[:, 0:16], lg3, axis=AX)
    vsync(mk, ve, ins)
    ins = ve.tensor_tensor(mk3, lg3, bc16(rstat[:, 0:16]), ALU.is_ge)
    vsync(mk, ve, ins)
    ins = ve.scalar_tensor_tensor(e1a[:], mka[:], -1e30, lgall[:], ALU.mult, ALU.add)
    vsync(mk, ve, ins)
    ins = ve.reduce_max(rstat[:, 16:32], e13, axis=AX)
    vsync(mk, ve, ins)
    ins = ve.tensor_tensor(e13, lg3, bc16(rstat[:, 0:16]), ALU.subtract)
    ins = ve.tensor_sub(rstat[:, 32:48], rstat[:, 16:32], rstat[:, 0:16])
    vc_pe = mk.inc(ins, "Vc")
    mk.wait(sc, "Vc", vc_pe)
    sc.activation(e2a[:], e1a[:], AF.Exp)
    ins = sc.activation(rstat[:, 48:64], rstat[:, 32:48], AF.Exp)
    sg_ex = mk.inc(ins, "Sg")
    mk.wait(ve, "Sg", sg_ex)
    ins = ve.tensor_scalar_add(rstat[:, 48:64], rstat[:, 48:64], 1.0)
    vsync(mk, ve, ins)
    ins = ve.reciprocal(rstat[:, 64:80], rstat[:, 48:64])
    vsync(mk, ve, ins)
    ins = ve.tensor_tensor(mk3, lg3, bc16(rstat[:, 16:32]), ALU.is_ge)
    vsync(mk, ve, ins)
    ins = ve.tensor_tensor(e23, e23, bc16(rstat[:, 64:80]), ALU.mult)
    vsync(mk, ve, ins)
    ins = ve.tensor_mul(wca[:], e2a[:], mka[:])
    vc_w = mk.inc(ins, "Vc")
    mk.wait(ve, "Vc", vc_w)
    mk.wait(te, "Vc", vc_w)
    for tb in range(16):
        mk.wait(te, "Vc", mk.ctr["Vc"])
        ins = te.transpose(pU[0:8, 0:128], wca[:, tb * 8:(tb + 1) * 8], idf[:])
        tmt = mk.inc(ins, "Tm")
        mk.wait(ve, "Tm", tmt)
        ve.tensor_copy(wT[0:8, tb * 128:(tb + 1) * 128], pU[0:8, 0:128])
        ins = ve.tensor_copy(wT16[0:8, tb * 128:(tb + 1) * 128], pU[0:8, 0:128])
        mk.inc(ins, "Vc")
    vc_router = mk.ctr["Vc"]
    mk.wait(sy, "Vc", vc_router)
    ins = sy.dma_start(wT_dr, wT[0:8, :])
    wt_d = mk.inc(ins, "Ad", 16)
    if dbg is not None:
        ins = sy.dma_start(dbg["wT"], wT[0:8, :])
        mk.inc(ins, "Od", 16)

    import os
    JOIN = os.environ.get('JOIN_GLU', '0') == '1'
    nc_engs = (te, ve, sc, gp, sy)
    if JOIN:
        mk.join(*nc_engs)
    for e in range(NE):
        for wbuf, wsrc in [(wg16, wd["gate"]), (wu16, wd["up"])]:
            for i in range(NKB):
                mk.wait(sy, "Wc", mk.ctr["Wc"])
                mk.wait(sy, "Tm", mk.ctr["Tm"])
                ins = sy.dma_start(wst[:, 0:1024], wsrc[e, i * 128:(i + 1) * 128, :])
                wv_ = mk.inc(ins, "Wd", 16)
                eng = sc if i % 2 == 0 else ve
                mk.wait(eng, "Tm", mk.ctr["Tm"])
                mk.wait(eng, "Wd", wv_)
                if eng is sc:
                    ins = eng.copy(wbuf[i][:], wst[:, 0:1024])
                else:
                    ins = eng.tensor_copy(wbuf[i][:], wst[:, 0:1024])
                mk.inc(ins, "Wc")
        wc_e = mk.ctr["Wc"]
        if JOIN:
            mk.join(*nc_engs)
        mk.wait(sy, "Ad", wt_d)
        mk.wait(sy, "Tm", mk.ctr["Tm"])
        ins = sy.dma_start(wrow[:], wT_dr[e:e + 1, :])
        wr_d = mk.inc(ins, "Ad", 16)
        mk.wait(te, "Ad", wr_d)
        mk.wait(te, "Wc", wc_e)
        for tc in range(4):
            if JOIN:
                mk.join(*nc_engs)
            ts4 = slice(tc * 512, (tc + 1) * 512)
            mk.wait(te, "Vc", mk.ctr["Vc"])
            ins = te.matmul(pdown[0], onesf_s[:], wrow[0:1, ts4], start=True, stop=True)
            tm_wb = mk.inc(ins, "Tm")
            for fb in range(NKB):
                if os.environ.get('JOIN_FB', '0') == '1':
                    mk.join(*nc_engs)
                mk.wait(te, "Vc", mk.ctr["Vc"])
                mk.wait(te, "Gc", mk.ctr["Gc"])
                for kb in range(NKB):
                    te.matmul(pG, wg16[kb][:, fb * 128:(fb + 1) * 128],
                              x16[kb][:, ts4], start=(kb == 0), stop=(kb == NKB - 1))
                for kb in range(NKB):
                    ins = te.matmul(pU, wu16[kb][:, fb * 128:(fb + 1) * 128],
                                    x16[kb][:, ts4], start=(kb == 0), stop=(kb == NKB - 1))
                tm_gu = mk.inc(ins, "Tm")
                mk.wait(ve, "Tm", tm_gu)
                col = e * 8 + fb
                mk.wait(ve, "Sg", mk.ctr["Sg"])
                mk.wait(ve, "Gc", mk.ctr["Gc"])
                ve.tensor_scalar(gst[:], pG, bgc[:, col:col + 1], 7.0, ALU.add, ALU.min)
                ins = ve.tensor_scalar(ust[:], pU, buc[:, col:col + 1], -7.0, ALU.add, ALU.max)
                vc_g = mk.inc(ins, "Vc")
                mk.wait(sc, "Vc", vc_g)
                ins = sc.activation(slst[:], gst[:], AF.Silu, scale=AL)
                sg_sl = mk.inc(ins, "Sg")
                ve.tensor_scalar(u2st[:], ust[:], 7.0, 1.0, ALU.min, ALU.add)
                mk.wait(ve, "Sg", sg_sl)
                ins = ve.tensor_mul(m1st[:], slst[:], u2st[:])
                mk.wait(ve, "Tm", tm_wb)
                mk.wait(ve, "Ad", mk.ctr["Ad"])
                ins = ve.tensor_mul(act16[:], m1st[:], pdown[0])
                vc_a = mk.inc(ins, "Vc")
                mk.wait(sy, "Vc", vc_a)
                ins = sy.dma_start(act_dr[e, fb * 128:(fb + 1) * 128, ts4], act16[:])
                mk.inc(ins, "Ad", 16)
                if dbg is not None and e == 0 and fb == 0 and tc == 0:
                    ins = sy.dma_start(dbg["act"], act16[:])
                    mk.inc(ins, "Od", 16)

    if os.environ.get('JOIN_DOWN', '0') == '1':
        mk.join(te, ve, sc, gp, sy)
    for e in range(NE):
        for i in range(NKB):
            mk.wait(sy, "Wc", mk.ctr["Wc"])
            ins = sy.dma_start(wst[:, 0:1024], wd["down"][e, i * 128:(i + 1) * 128, :])
            wv_ = mk.inc(ins, "Wd", 16)
            eng = sc if i % 2 == 0 else ve
            mk.wait(eng, "Tm", mk.ctr["Tm"])
            mk.wait(eng, "Wd", wv_)
            mk.wait(eng, "Ad", mk.ctr["Ad"])
            if eng is sc:
                ins = eng.copy(wg16[0][:], wst[:, 0:1024])
            else:
                ins = eng.tensor_copy(wg16[0][:], wst[:, 0:1024])
            cc = mk.inc(ins, "Wc")
            mk.wait(sy, "Wc", cc)
            ins = sy.dma_start(down_dr[e, i * 128:(i + 1) * 128, :], wg16[0][:])
            mk.inc(ins, "Ad", 16)
    ad_all = mk.ctr["Ad"]

    mk.wait(sy, "Ad", ad_all)
    for tc in range(4):
        ts4 = slice(tc * 512, (tc + 1) * 512)
        mk.wait(sy, "Vc", mk.ctr["Vc"])
        for i in range(NKB):
            ins = sy.dma_start(xres[i][:], x3_dram[i * 128:(i + 1) * 128, ts4])
            xd_r = mk.inc(ins, "Xd", 16)
        for hhalf in range(2):
            mk.wait(te, "Vc", max(vc_db, mk.ctr["Vc"]) if False else vc_db)
            mk.wait(te, "Vc", mk.ctr["Vc"])
            mk.wait(te, "Sg", mk.ctr["Sg"])
            for m4 in range(4):
                mb = hhalf * 4 + m4
                ins = te.matmul(pdown[m4], dB[0:8, mb * 128:(mb + 1) * 128],
                                wT16[0:8, ts4], start=True, stop=True)
            tm_dn = mk.inc(ins, "Tm")
            mk.wait(sc, "Tm", tm_dn)
            mk.wait(sc, "Vc", mk.ctr["Vc"])
            mk.wait(sc, "Od", mk.ctr["Od"])
            for m4 in range(4):
                mb = hhalf * 4 + m4
                ins = sc.copy(rbuf[mb][:], pdown[m4])
                mk.inc(ins, "Sg")
            sg_bias = mk.ctr["Sg"]
            for e in range(NE):
                mk.wait(sy, "Tm", mk.ctr["Tm"])
                for fb in range(NKB):
                    ins = sy.dma_start(wdn8[fb][:], down_dr[e, fb * 128:(fb + 1) * 128,
                                                            hhalf * 512:(hhalf + 1) * 512])
                    wv_ = mk.inc(ins, "Wd", 16)
                    ins = sy.dma_start(acst8[fb][:], act_dr[e, fb * 128:(fb + 1) * 128, ts4])
                    ad_ = mk.inc(ins, "Ad", 16)
                mk.wait(te, "Wd", wv_)
                mk.wait(te, "Ad", ad_)
                mk.wait(te, "Sg", sg_bias)
                mk.wait(te, "Vc", mk.ctr["Vc"])
                for fb in range(NKB):
                    for m4 in range(4):
                        ins = te.matmul(pdown[m4], wdn8[fb][:, m4 * 128:(m4 + 1) * 128],
                                        acst8[fb][:], start=(fb == 0), stop=(fb == NKB - 1))
                tm_e = mk.inc(ins, "Tm")
                mk.wait(ve, "Tm", tm_e)
                mk.wait(ve, "Sg", sg_bias)
                for m4 in range(4):
                    mb = hhalf * 4 + m4
                    ins = ve.tensor_add(rbuf[mb][:], rbuf[mb][:], pdown[m4])
                mk.inc(ins, "Vc")
        sg_evac = mk.ctr["Sg"]
        if dbg is not None:
            mk.wait(sy, "Sg", sg_evac)
            for mb in range(NKB):
                ins = sy.dma_start(r_dr[mb * 128:(mb + 1) * 128, ts4], rbuf[mb][:])
                mk.inc(ins, "Rd", 16)
            mk.wait(ve, "Rd", mk.ctr["Rd"])
        mk.wait(ve, "Sg", sg_evac)
        mk.wait(ve, "Xd", xd_r)
        for mb in range(NKB):
            ins = ve.tensor_add(rbuf[mb][:], rbuf[mb][:], xres[mb][:])
        mk.inc(ins, "Vc")
        for half2 in range(2):
            cs = slice(half2 * 256, half2 * 256 + 256)
            vc_apply = build_ln(mk, ln, [r[:, cs] for r in rbuf], [r[:, cs] for r in rbuf])
            mk.wait(sy, "Vc", vc_apply)
            for i in range(NKB):
                ins = sy.dma_start(out_dram[i * 128:(i + 1) * 128,
                                            tc * 512 + half2 * 256:tc * 512 + half2 * 256 + 256],
                                   rbuf[i][:, cs])
                mk.inc(ins, "Od", 16)
    if dbg is not None:
        mk.join(te, ve, sc, gp, sy)
        for i in range(NKB):
            ins = sy.dma_start(dbg["x16d"][:, i * T:(i + 1) * T], x16[i][:])
            mk.inc(ins, "Od", 16)
    mk.wait(sy, "Od", mk.ctr["Od"])


def host_moe_weights(router_weight, gate_up_proj, gate_up_proj_bias, down_proj,
                     down_proj_bias):
    gate = np.ascontiguousarray(gate_up_proj[:, :, 0::2])
    up = np.ascontiguousarray(gate_up_proj[:, :, 1::2])
    bg = gate_up_proj_bias[:, 0::2]
    bu = gate_up_proj_bias[:, 1::2]
    bgc = np.zeros((128, 64), np.float32)
    buc = np.zeros((128, 64), np.float32)
    for e in range(8):
        for fb in range(8):
            bgc[:, e * 8 + fb] = bg[e, fb * 128:(fb + 1) * 128]
            buc[:, e * 8 + fb] = bu[e, fb * 128:(fb + 1) * 128]
    return dict(router=np.ascontiguousarray(router_weight),
                gate=gate, up=up, bgc=bgc, buc=buc,
                down=np.ascontiguousarray(down_proj),
                dbias=np.ascontiguousarray(down_proj_bias),
                idf=np.eye(128, dtype=np.float32))



def build_l0_phase(mk, x0_dram, x1_dram, wd, dbg=None):
    import os
    STAGE = int(os.environ.get("L0_STAGE", "9"))
    nc = mk.nc
    sy, te, ve, sc, gp = nc.sync, nc.tensor, nc.vector, nc.scalar, nc.gpsimd
    SC = 256
    NSC = int(os.environ.get("L0_NSC", str(T // SC)))
    NH = 16

    x0T = [mk.sb(f"l0_x0_{i}", [128, T], F32) for i in range(NKB)]
    maac = {g: mk.sb(f"l0_maa_{g}", [128, 8], F32) for g in "xwkvrg"}
    tdc = mk.sb("l0_tdc", [128, 8], F32)
    uc = mk.sb("l0_uc", [128, 8], F32)
    gamr = mk.sb("l0_gamr", [1, 1024], F32)
    betr = mk.sb("l0_betr", [1, 1024], F32)
    gbc = mk.sb("l0_gbc", [128, 1024], F32)
    bbc = mk.sb("l0_bbc", [128, 1024], F32)
    maaw1 = [mk.sb(f"l0_mw1_{i}", [128, 160], BF16) for i in range(NKB)]
    maaw2 = mk.sb("l0_mw2", [32, 5 * 1024], BF16)
    dw1 = mk.sb("l0_dw1", [128, 8 * 64], BF16)
    dw2 = mk.sb("l0_dw2", [64, 1024], BF16)
    hsel = mk.sb("l0_hsel", [128, 128], BF16)
    i64t = mk.sb("l0_i64t", [64, 1024], BF16)
    ust = mk.sb("l0_ust", [64, 512], BF16)
    rmask = mk.sb("l0_rmask", [128, 256], F32)
    idf = mk.sb("l0_idf", [128, 128], F32)
    id16 = mk.sb("l0_id16", [128, 128], BF16)
    wstf = mk.sb("l0_wstf", [128, 512], F32)
    wcb = [mk.sb(f"l0_wcb_{i}", [128, 1024], BF16) for i in range(2)]
    wc16 = wcb[0]
    vgw = [mk.sb(f"l0_vgw_{i}", [128, 1024], BF16) for i in range(NKB)]
    xxT = [vgw[i][:].bitcast(F32)[:, 0:SC] for i in range(NKB)]
    xsb = [vgw[i][:].bitcast(F32)[:, SC:2 * SC] for i in range(NKB)]
    pcol = mk.sb("l0_pcol", [128, 8], F32)
    Xg16 = [mk.sb(f"l0_xg_{i}", [128, SC], BF16) for i in range(NKB)]
    Xv16 = [mk.sb(f"l0_xv_{i}", [128, SC], BF16) for i in range(NKB)]
    Xgg16 = [mk.sb(f"l0_xgg_{i}", [128, SC], BF16) for i in range(NKB)]
    mt16 = [mk.sb(f"l0_mt_{i}", [32, SC], BF16) for i in range(5)]
    td16 = mk.sb("l0_td16", [64, SC], BF16)
    r16 = [mk.sb(f"l0_r16_{i}", [128, SC], BF16) for i in range(NKB)]
    k16 = [mk.sb(f"l0_k16_{i}", [128, SC], BF16) for i in range(NKB)]
    Kt16 = [mk.sb(f"l0_kt16_{i}", [128, SC], BF16) for i in range(NKB)]
    eWT = [mk.sb(f"l0_ew_{i}", [128, SC], F32) for i in range(NKB)]
    cT = [mk.sb(f"l0_c_{i}", [128, SC], F32) for i in range(NKB)]
    ctot4 = [mk.sb(f"l0_ct4_{i}", [128, 4], F32) for i in range(NKB)]
    ect4 = [mk.sb(f"l0_ec4_{i}", [128, 4], F32) for i in range(NKB)]
    wtE4 = mk.sb("l0_wte4", [128, 32], F32)
    vA = [mk.sb(f"l0_va_{i}", [64, 1024], BF16) for i in range(4)]
    gA = [mk.sb(f"l0_ga_{i}", [128, 1024], BF16) for i in range(2)]
    rku16 = [mk.sb(f"l0_rku_{i}", [128, SC], BF16) for i in range(NKB)]
    dsb = mk.sb("l0_dsb", [16, SC], F32)
    dcm = mk.sb("l0_dcm", [64, 64], F32)
    PT16 = mk.sb("l0_pt16", [64, 1024], BF16)
    Dg16 = mk.sb("l0_dg16", [64, 1024], BF16)
    KtA = mk.sb("l0_kta", [64, 1024], BF16)
    S = mk.sb("l0_S", [128, 512], F32)
    S16 = mk.sb("l0_S16", [128, 512], BF16)
    y2 = mk.sb("l0_y2", [128, 1024], F32)
    ybsb = mk.sb("l0_ybsb", [128, 1024], F32)
    ysq = ybsb
    gns = mk.sb("l0_gns", [128, 96], F32)


    ygT16 = Xv16
    rbuf = eWT
    PJ = [mk.ps(f"l0_pj_{i}", [128, 512])[:, :] for i in range(2)]
    PA = mk.ps("l0_pa", [128, 1024])[:, :]
    PB = mk.ps("l0_pb", [128, 1024])[:, :]
    PSb = [mk.ps(f"l0_ps_{i}", [64, 512])[:, :] for i in range(2)]
    ln = alloc_ln_scratch(mk, SC, "l0ln", pst=PSb[0][0:1, 0:2 * SC], pbcb=PJ[0],
                          r16s=[cT[i][:].bitcast(BF16)[:, 0:SC] for i in range(NKB)])
    wscr = mk.dtmp("l0_wscr", [5, D, D], BF16)

    init_ln_scratch(mk, ln)
    for g in "xwkvrg":
        ins = sy.dma_start(maac[g][:], wd["maa_" + g]); mk.inc(ins, "Wd", 16)
    for nm, dst in [("tdc", tdc), ("uc", uc), ("gamr", gamr), ("betr", betr),
                    ("hsel", hsel), ("i64t", i64t), ("ust", ust), ("rmask", rmask),
                    ("idf", idf), ("id16", id16)]:
        ins = sy.dma_start(dst[:], wd[nm]); mk.inc(ins, "Wd", 16)
    wd_const = mk.ctr["Wd"]
    small = []
    for i in range(NKB):
        small.append((wd["maaw1"][i * 128:(i + 1) * 128, :], maaw1[i][:], 128, 160))
    for g5 in range(5):
        small.append((wd["maaw2"][g5], maaw2[:, g5 * 1024:(g5 + 1) * 1024], 32, 1024))
    for i in range(NKB):
        small.append((wd["dw1"][i * 128:(i + 1) * 128, :], dw1[:, i * 64:(i + 1) * 64], 128, 64))
    small.append((wd["dw2"], dw2[:], 64, 1024))
    for (dsl, dst, npart, ncols) in small:
        for q0 in range(0, ncols, 512):
            qn = min(512, ncols - q0)
            mk.wait(sy, "Wc", mk.ctr["Wc"])
            ins = sy.dma_start(wstf[0:npart, 0:qn], dsl[:, q0:q0 + qn])
            wv_ = mk.inc(ins, "Wd", 16)
            mk.wait(ve, "Wd", wv_)
            ins = ve.tensor_copy(dst[:, q0:q0 + qn], wstf[0:npart, 0:qn])
            mk.inc(ins, "Wc")
    for wi, wn in enumerate(["W_r", "W_k", "W_v", "W_g", "W_o"]):
        for i in range(NKB):
            for q0 in range(0, 1024, 512):
                mk.wait(sy, "Wc", mk.ctr["Wc"])
                ins = sy.dma_start(wstf[:], wd[wn][i * 128:(i + 1) * 128, q0:q0 + 512])
                wv_ = mk.inc(ins, "Wd", 16)
                eng = sc if i % 2 == 0 else ve
                mk.wait(eng, "Wd", wv_)
                mk.wait(eng, "Ad", mk.ctr["Ad"])
                dstw = wc16[:, q0:q0 + 512]
                if wn == "W_o":
                    ins = eng.mul(dstw, wstf[:], 0.5) if eng is sc else \
                          eng.tensor_scalar_mul(dstw, wstf[:], 0.5)
                else:
                    ins = eng.copy(dstw, wstf[:]) if eng is sc else \
                          eng.tensor_copy(dstw, wstf[:])
                mk.inc(ins, "Wc")
            cc_ = mk.ctr["Wc"]
            mk.wait(gp, "Wc", cc_)
            ins = gp.dma_start(wscr[wi, i * 128:(i + 1) * 128, :], wc16[:])
            mk.inc(ins, "Ad", 16)
    ad_wscr = mk.ctr["Ad"]
    mk.wait(te, "Wd", wd_const)
    mk.wait(te, "Wc", mk.ctr["Wc"])
    for src_r, dst_b in [(gamr, gbc), (betr, bbc)]:
        for half in range(2):
            mk.wait(te, "Vc", mk.ctr["Vc"])
            ins = te.matmul(PJ[1][:, 0:512], ln["onesf"][:],
                            src_r[0:1, half * 512:(half + 1) * 512], start=True, stop=True)
            tmg = mk.inc(ins, "Tm")
            mk.wait(ve, "Tm", tmg)
            ins = ve.tensor_copy(dst_b[:, half * 512:(half + 1) * 512], PJ[1][:, 0:512])
            mk.inc(ins, "Vc")
    ins = ve.memset(pcol[:], 0.0)
    vsync(mk, ve, ins)
    ins = ve.memset(S[:], 0.0)
    vsync(mk, ve, ins)
    ins = ve.memset(S16[:], 0.0)
    vsync(mk, ve, ins)

    for i in range(NKB):
        ins = gp.dma_start(x0T[i][:], x0_dram[i * 128:(i + 1) * 128, :])
        mk.inc(ins, "Xd", 16)
    xd_x0 = mk.ctr["Xd"]
    mk.wait(ve, "Xd", xd_x0)
    mk.wait(te, "Xd", xd_x0)
    mk.wait(sy, "Ad", ad_wscr)

    def stream_wcol(wi, mb, j):
        sl = wscr[wi][:, mb * 128:(mb + 1) * 128].rearrange("(kb p) c -> p kb c", p=128)
        dst = wcb[j][:].rearrange("p (kb c) -> p kb c", c=128)
        ins = sy.dma_start(dst, sl)
        return mk.inc(ins, "Wd", 16)

    for sci in range(NSC):
        t0 = sci * SC
        ts4 = slice(t0, t0 + SC)
        mk.wait(ve, "Tm", mk.ctr["Tm"])
        mk.wait(ve, "Wd", mk.ctr["Wd"])
        for i in range(NKB):
            ve.tensor_sub(xxT[i][:, 1:SC], x0T[i][:, t0:t0 + SC - 1], x0T[i][:, t0 + 1:t0 + SC])
            ins = ve.tensor_sub(xxT[i][:, 0:1], pcol[:, i:i + 1], x0T[i][:, t0:t0 + 1])
            vsync(mk, ve, ins)
            ins = ve.tensor_copy(pcol[:, i:i + 1], x0T[i][:, t0 + SC - 1:t0 + SC])
            vsync(mk, ve, ins)
            ins = ve.scalar_tensor_tensor(Xg16[i][:], xxT[i][:], maac["x"][:, i:i + 1],
                                          x0T[i][:, ts4], ALU.mult, ALU.add)
        vc_xxx = mk.inc(ins, "Vc")
        if dbg is not None and sci == 0:
            mk.wait(gp, "Vc", vc_xxx)
            for i_ in range(NKB):
                ins = gp.dma_start(dbg["xx"][i_ * 128:(i_ + 1) * 128, :], xxT[i_][:])
                mk.inc(ins, "Od", 16)
            ins = gp.dma_start(dbg["xxx"], Xg16[0][:])
            od_xx = mk.inc(ins, "Od", 16)
            mk.wait(ve, "Od", od_xx)
            mk.wait(sy, "Od", od_xx)
        mk.wait(te, "Vc", vc_xxx)
        mk.wait(te, "Sg", mk.ctr["Sg"])
        for (c0, cn, pj) in [(0, 64, PSb[0][0:64, 0:SC]), (64, 64, PSb[1][0:64, 0:SC]),
                             (128, 32, PJ[1][0:32, 0:SC])]:
            for kb in range(NKB):
                ins = te.matmul(pj, maaw1[kb][:, c0:c0 + cn], Xg16[kb][:],
                                start=(kb == 0), stop=(kb == NKB - 1))
            tm_m = mk.inc(ins, "Tm")
            mk.wait(sc, "Tm", tm_m)
            for h2 in range(cn // 32):
                g5 = (c0 + h2 * 32) // 32
                ins = sc.activation(mt16[g5][:], pj[h2 * 32:h2 * 32 + 32, :], AF.Tanh)
            mk.inc(ins, "Sg")
        sg_mt = mk.ctr["Sg"]
        if dbg is not None and sci == 0:
            mk.wait(gp, "Sg", sg_mt)
            for g5 in range(5):
                ins = gp.dma_start(dbg["mt"][:, g5 * 256:(g5 + 1) * 256], mt16[g5][:])
                mk.inc(ins, "Od", 16)
            od_mt = mk.inc(gp.engine_nop(), "Gc") if False else mk.ctr["Od"]
            mk.wait(sc, "Od", od_mt)
            mk.wait(te, "Od", od_mt)

        def assemble_gate(gi, g, dst16):
            mk.wait(te, "Sg", sg_mt)
            for mb in range(NKB):
                pj = PJ[mb % 2][:, 0:SC]
                mk.wait(te, "Vc", mk.ctr["Vc"])
                ins = te.matmul(pj, maaw2[:, gi * 1024 + mb * 128:gi * 1024 + (mb + 1) * 128],
                                mt16[gi][:], start=True, stop=True)
                tm_mg = mk.inc(ins, "Tm")
                mk.wait(ve, "Tm", tm_mg)
                ve.scalar_tensor_tensor(xsb[mb][:], pj, maac[g][:, mb:mb + 1], xxT[mb][:],
                                        ALU.add, ALU.mult)
                ins = ve.tensor_add(dst16[mb][:], xsb[mb][:], x0T[mb][:, ts4])
                mk.inc(ins, "Vc")
            return mk.ctr["Vc"]

        vc_g = assemble_gate(0, "w", Xg16)
        mk.wait(te, "Vc", vc_g)
        for kb in range(NKB):
            ins = te.matmul(PSb[0][0:64, 0:SC], dw1[:, kb * 64:(kb + 1) * 64], Xg16[kb][:],
                            start=(kb == 0), stop=(kb == NKB - 1))
        tm_t1 = mk.inc(ins, "Tm")
        mk.wait(sc, "Tm", tm_t1)
        ins = sc.activation(td16[:], PSb[0][0:64, 0:SC], AF.Tanh)
        sg_t1 = mk.inc(ins, "Sg")
        mk.wait(te, "Sg", sg_t1)
        for mb in range(NKB):
            pj = PJ[mb % 2][:, 0:SC]
            mk.wait(te, "Sg", mk.ctr["Sg"])
            mk.wait(te, "Vc", mk.ctr["Vc"])
            ins = te.matmul(pj, dw2[:, mb * 128:(mb + 1) * 128], td16[:], start=True, stop=True)
            tm_d = mk.inc(ins, "Tm")
            mk.wait(sc, "Tm", tm_d)
            mk.wait(sc, "Vc", mk.ctr["Vc"])
            mk.wait(sc, "Od", mk.ctr["Od"])
            ins = sc.activation(eWT[mb][:], pj, AF.Exp, bias=tdc[:, mb:mb + 1])
            mk.inc(ins, "Sg")
        sg_eW = mk.ctr["Sg"]

        for g, dstP in [("k", k16), ("r", r16)]:
            gi = {"k": 1, "r": 3}[g]
            wi = {"k": 1, "r": 0}[g]
            vc_g = assemble_gate(gi, g, Xg16)
            mk.wait(te, "Vc", vc_g)
            for mb in range(NKB):
                j = mb % 2
                mk.wait(sy, "Tm", mk.ctr["Tm"] if mb < 2 else tm_wp[j])
                wv_ = stream_wcol(wi, mb, j)
                pj = PJ[mb % 2][:, 0:SC]
                mk.wait(te, "Wd", wv_)
                mk.wait(te, "Sg", mk.ctr["Sg"])
                for kb in range(NKB):
                    ins = te.matmul(pj, wcb[j][:, kb * 128:(kb + 1) * 128], Xg16[kb][:],
                                    start=(kb == 0), stop=(kb == NKB - 1))
                if mb < 2:
                    tm_wp = [0, 0]
                tm_wp[j] = mk.inc(ins, "Tm")
                mk.wait(sc, "Tm", tm_wp[j])
                ins = sc.copy(dstP[mb][:], pj)
                sg_pr = mk.inc(ins, "Sg")
                if dbg is not None and sci == 0:
                    mk.wait(gp, "Sg", sg_pr)
                    ins = gp.dma_start(dbg[g + "16"][mb * 128:(mb + 1) * 128, :], dstP[mb][:])
                    od_pr = mk.inc(ins, "Od", 16)
                    mk.wait(sc, "Od", od_pr)

        if STAGE < 2:
            continue
        assemble_gate(2, "v", Xv16)
        vc_g = assemble_gate(4, "g", Xgg16)
        for wi, X16, is_v in [(2, Xv16, True), (3, Xgg16, False)]:
            mk.wait(sy, "Vc", mk.ctr["Vc"])
            mk.wait(sy, "Tm", mk.ctr["Tm"])
            for kb in range(NKB):
                ins = sy.dma_start(vgw[kb][:], wscr[wi, kb * 128:(kb + 1) * 128, :])
                wv_ = mk.inc(ins, "Wd", 16)
            mk.wait(te, "Wd", wv_)
            mk.wait(te, "Vc", mk.ctr["Vc"])
            if is_v:
                for cc in range(4):
                    for half in range(2):
                        pj = PSb[half][0:64, 0:512]
                        mk.wait(te, "Sg", mk.ctr["Sg"])
                        for kb in range(NKB):
                            ins = te.matmul(pj, X16[kb][:, cc * 64:(cc + 1) * 64],
                                            vgw[kb][:, half * 512:(half + 1) * 512],
                                            start=(kb == 0), stop=(kb == NKB - 1))
                        tm_v = mk.inc(ins, "Tm")
                        mk.wait(sc, "Tm", tm_v)
                        ins = sc.copy(vA[cc][:, half * 512:(half + 1) * 512], pj)
                        mk.inc(ins, "Sg")
            else:
                for tb2 in range(2):
                    for half in range(2):
                        pj = PJ[half][:, 0:512]
                        mk.wait(te, "Sg", mk.ctr["Sg"])
                        mk.wait(te, "Vc", mk.ctr["Vc"])
                        for kb in range(NKB):
                            ins = te.matmul(pj, X16[kb][:, tb2 * 128:(tb2 + 1) * 128],
                                            vgw[kb][:, half * 512:(half + 1) * 512],
                                            start=(kb == 0), stop=(kb == NKB - 1))
                        tm_g = mk.inc(ins, "Tm")
                        mk.wait(sc, "Tm", tm_g)
                        mk.wait(sc, "Vc", mk.ctr["Vc"])
                        ins = sc.activation(wc16[:, 0:512].bitcast(F32)[:, 0:512] if False
                                            else wstf[:, 0:512], pj, AF.Tanh, scale=0.5)
                        sg_th = mk.inc(ins, "Sg")
                        mk.wait(ve, "Sg", sg_th)
                        mk.wait(ve, "Tm", tm_g)
                        ins = ve.scalar_tensor_tensor(gA[tb2][:, half * 512:(half + 1) * 512],
                                                      wstf[:, 0:512], 1.0, pj,
                                                      ALU.add, ALU.mult)
                        mk.inc(ins, "Vc")

        if dbg is not None and sci == 0 and STAGE >= 2:
            mk.wait(gp, "Sg", mk.ctr["Sg"])
            mk.wait(gp, "Vc", mk.ctr["Vc"])
            for c_ in range(4):
                ins = gp.dma_start(dbg["va"][c_ * 64:(c_ + 1) * 64, :], vA[c_][:])
                mk.inc(ins, "Od", 16)
            ins = gp.dma_start(dbg["ga"], gA[0][:])
            od_g = mk.inc(ins, "Od", 16)
            mk.wait(ve, "Od", od_g)
        if STAGE < 3:
            continue
        mk.wait(ve, "Sg", sg_eW)
        for mb in range(NKB):
            ins = ve.tensor_tensor_scan(cT[mb][:], rmask[:], eWT[mb][:], 0.0,
                                        ALU.mult, ALU.add)
        vc_c = mk.inc(ins, "Vc")
        mk.wait(ve, "Vc", vc_c)
        for mb in range(NKB):
            ve.tensor_copy(ctot4[mb][:], cT[mb][:].rearrange("p (c l) -> p c l", l=64)[:, :, 63:64])
            ins = ve.tensor_sub(eWT[mb][:], eWT[mb][:], cT[mb][:])
        vc_t1 = mk.inc(ins, "Vc")
        mk.wait(sc, "Vc", vc_t1)
        for mb in range(NKB):
            sc.activation(eWT[mb][:], eWT[mb][:], AF.Exp)
            sc.activation(cT[mb][:], cT[mb][:], AF.Exp)
            ins = sc.activation(ect4[mb][:], ctot4[mb][:], AF.Exp, scale=-1.0)
        sg_E = mk.inc(ins, "Sg")
        mk.wait(ve, "Sg", sg_E)
        for mb in range(NKB):
            ve.scalar_tensor_tensor(rku16[mb][:], r16[mb][:], uc[:, mb:mb + 1],
                                    k16[mb][:], ALU.mult, ALU.mult)
            ve.tensor_mul(r16[mb][:], r16[mb][:], eWT[mb][:])
            ve.tensor_mul(k16[mb][:], k16[mb][:], cT[mb][:])
            ect_b = ect4[mb][:].rearrange("p (c o) -> p c o", o=1).broadcast_to([128, 4, 64])
            ve.tensor_tensor(Kt16[mb][:].rearrange("p (c l) -> p c l", l=64),
                             k16[mb][:].rearrange("p (c l) -> p c l", l=64), ect_b, ALU.mult)
            ins = ve.tensor_copy(wtE4[:, mb:32:8], ect4[mb][:])
            vsync(mk, ve, ins)
        vc_fac = mk.inc(ins, "Vc") if False else mk.ctr["Vc"]
        if dbg is not None and sci == 0:
            mk.wait(gp, "Vc", vc_fac)
            for i_ in range(NKB):
                ins = gp.dma_start(dbg["qt"][i_ * 128:(i_ + 1) * 128, :], r16[i_][:])
                mk.inc(ins, "Od", 16)
                ins = gp.dma_start(dbg["kt"][i_ * 128:(i_ + 1) * 128, :], k16[i_][:])
                mk.inc(ins, "Od", 16)
            od_f = mk.ctr["Od"]
            mk.wait(ve, "Od", od_f)
            mk.wait(sc, "Od", od_f)
        mk.wait(te, "Vc", vc_fac)
        mk.wait(te, "Sg", mk.ctr["Sg"])
        for kb in range(NKB):
            ins = te.matmul(PSb[1][0:16, 0:SC], hsel[:, kb * 16:(kb + 1) * 16], rku16[kb][:],
                            start=(kb == 0), stop=(kb == NKB - 1))
        tm_dd = mk.inc(ins, "Tm")
        mk.wait(ve, "Tm", tm_dd)
        ins = ve.tensor_copy(dsb[:], PSb[1][0:16, 0:SC])
        vc_dsb = mk.inc(ins, "Vc")
        mk.wait(te, "Vc", vc_dsb)
        for cc in range(4):
            ins = te.transpose(PSb[1][0:64, cc * 16:(cc + 1) * 16], dsb[:, cc * 64:(cc + 1) * 64],
                               idf[0:16, 0:16])
        tm_dt = mk.inc(ins, "Tm")
        mk.wait(ve, "Tm", tm_dt)
        ins = ve.tensor_copy(dcm[:], PSb[1][0:64, 0:64])
        vc_dcm = mk.inc(ins, "Vc")

        if STAGE < 4:
            continue
        for cc in range(4):
            cs = slice(cc * 64, (cc + 1) * 64)
            p2 = cc % 2
            mk.wait(te, "Vc", mk.ctr["Vc"])
            for h in range(NH):
                mt = h // 2
                hr = slice(64 * (h % 2), 64 * (h % 2) + 64)
                ins = te.matmul(PSb[h % 2][0:64, 64 * (h // 2):64 * (h // 2) + 64],
                                k16[mt][hr, cs], r16[mt][hr, cs], start=True, stop=True)
            tm_sc = mk.inc(ins, "Tm")
            if int(os.environ.get("L0_SUB", "9")) < 2:
                mk.wait(ve, "Tm", tm_sc)
                continue
            mk.wait(ve, "Vc", vc_dcm)
            dcs = dcm[:, cc * 16:(cc + 1) * 16]
            for par in range(2):
                dsl = dcs[:, par::2].rearrange("p (j o) -> p j o", o=1).broadcast_to([64, 8, 64])
                ins = ve.tensor_tensor(
                    Dg16[:, par * 512:(par + 1) * 512].rearrange("p (j l) -> p j l", l=64),
                    i64t[:, 0:512].rearrange("p (j l) -> p j l", l=64), dsl, ALU.mult)
            gc_dg = mk.inc(ins, "Gc")
            mk.wait(ve, "Tm", tm_sc)
            for b in range(2):
                ins = ve.tensor_mul(PT16[:, b * 512:(b + 1) * 512], PSb[b][0:64, 0:512], ust[:])
            mk.inc(ins, "Vc")
            mk.wait(ve, "Gc", gc_dg)
            if int(os.environ.get("L0_SUB", "9")) >= 3:
                ins = ve.tensor_add(PT16[:], PT16[:], Dg16[:])
            vc_pt = mk.inc(ins, "Vc")
            if dbg is not None and sci == 0 and cc == 0:
                mk.wait(gp, "Vc", vc_pt)
                ins = gp.dma_start(dbg["pt0"], PT16[:])
                od_p = mk.inc(ins, "Od", 16)
                mk.wait(te, "Od", od_p)
            if STAGE < 5:
                continue
            mk.wait(te, "Vc", vc_pt)
            mk.wait(te, "Sg", mk.ctr["Sg"])
            mk.wait(te, "Vc", mk.ctr["Vc"])
            yrow = slice(64 * p2, 64 * p2 + 64)
            for h in range(NH):
                hc = slice(64 * h, 64 * h + 64)
                pc = slice(512 * (h % 2) + 64 * (h // 2), 512 * (h % 2) + 64 * (h // 2) + 64)
                ins = te.matmul(PA[yrow, hc], PT16[:, pc], vA[cc][:, hc], start=True, stop=True)
            for h in range(NH):
                mt = h // 2
                hr = slice(64 * (h % 2), 64 * (h % 2) + 64)
                pc = slice(512 * (h % 2) + 64 * (h // 2), 512 * (h % 2) + 64 * (h // 2) + 64)
                ins = te.matmul(PB[yrow, pc], r16[mt][hr, cs],
                                S16[hr, 64 * (h // 2):64 * (h // 2) + 64], start=True, stop=True)
            tm_y = mk.inc(ins, "Tm")
            SKIPDS = os.environ.get("L0_SKIPDS", "0") == "1"
            if SKIPDS:
                pass
            mk.wait(te, "Sg", mk.ctr["Sg"])
            if not SKIPDS:
                for mb in range(NKB):
                    ins = te.transpose(PJ[1].bitcast(BF16)[0:64, mb * 128:(mb + 1) * 128],
                                       Kt16[mb][:, cs], id16[:])
                tm_kt = mk.inc(ins, "Tm")
            if not SKIPDS:
                mk.wait(sc, "Tm", tm_kt)
                ins = sc.copy(KtA[:], PJ[1].bitcast(BF16)[0:64, 0:1024])
                sg_kta = mk.inc(ins, "Sg")
            if not SKIPDS:
                mk.wait(te, "Sg", sg_kta)
                mk.wait(te, "Vc", mk.ctr["Vc"])
                for h in range(NH):
                    hr = slice(64 * (h % 2), 64 * (h % 2) + 64)
                    ins = te.matmul(PJ[0][hr, 64 * (h // 2):64 * (h // 2) + 64],
                                    KtA[:, 64 * h:64 * h + 64], vA[cc][:, 64 * h:64 * h + 64],
                                    start=True, stop=True)
                tm_ds = mk.inc(ins, "Tm")
            if not SKIPDS:
                wtb = wtE4[:, cc * 8:(cc + 1) * 8].rearrange("p (g o) -> p g o", o=1) \
                    .broadcast_to([128, 8, 64])
                mk.wait(ve, "Tm", tm_ds)
                ve.tensor_tensor(S[:].rearrange("p (g l) -> p g l", l=64),
                                 S[:].rearrange("p (g l) -> p g l", l=64), wtb, ALU.mult)
                ve.tensor_add(S[:], S[:], PJ[0])
                ins = ve.tensor_copy(S16[:], S[:])
                vc_su = mk.inc(ins, "Vc")
                if dbg is not None and sci == 0 and cc == 0:
                    mk.wait(gp, "Vc", vc_su)
                    ins = gp.dma_start(dbg["s0"], S[:])
                    od_s = mk.inc(ins, "Od", 16)
                    mk.wait(ve, "Od", od_s)
            if STAGE >= 6:
                mk.wait(sc, "Tm", tm_y)
                mk.wait(sc, "Vc", mk.ctr["Vc"])
                for h_ in range(16):
                    pc_ = 512 * (h_ % 2) + 64 * (h_ // 2)
                    ins = sc.copy(ybsb[yrow, 64 * h_:64 * h_ + 64],
                                  PB[yrow, pc_:pc_ + 64])
                sg_yb = mk.inc(ins, "Sg")
                mk.wait(ve, "Sg", sg_yb)
                mk.wait(ve, "Tm", tm_y)
                ve.tensor_tensor(y2[yrow, 0:512], ybsb[yrow, 0:512], PA[yrow, 0:512], ALU.add)
                ins = ve.tensor_tensor(y2[yrow, 512:1024], ybsb[yrow, 512:1024],
                                       PA[yrow, 512:1024], ALU.add)
                vc_y2 = mk.inc(ins, "Vc")
            if p2 == 1 and STAGE >= 6:
                if dbg is not None and sci == 0 and cc == 1:
                    mk.wait(gp, "Vc", vc_y2)
                    ins = gp.dma_start(dbg["y2r"], y2[:])
                    od_yr = mk.inc(ins, "Od", 16)
                    mk.wait(sc, "Od", od_yr)
                    mk.wait(ve, "Od", od_yr)
                mk.wait(sc, "Vc", vc_y2)
                ins = sc.activation(ysq[:], y2[:], AF.Square)
                sg_sq = mk.inc(ins, "Sg")
                y3 = y2[:].rearrange("p (h l) -> p h l", l=64)
                ins = ve.reduce_sum(gns[:, 0:16], y3, axis=AX)
                vsync(mk, ve, ins)
                mk.wait(ve, "Sg", sg_sq)
                ins = ve.reduce_sum(gns[:, 16:32], ysq[:].rearrange("p (h l) -> p h l", l=64),
                                    axis=AX)
                vsync(mk, ve, ins)
                ins = ve.tensor_scalar_mul(gns[:, 32:48], gns[:, 0:16], 1.0 / 64)
                vsync(mk, ve, ins)
                ins = ve.tensor_mul(gns[:, 80:96], gns[:, 32:48], gns[:, 32:48])
                ve.tensor_scalar(gns[:, 48:64], gns[:, 16:32], 1.0 / 64, EPS_GN,
                                 ALU.mult, ALU.add)
                vsync(mk, ve, ins)
                ins = ve.tensor_sub(gns[:, 48:64], gns[:, 48:64], gns[:, 80:96])
                vsync(mk, ve, ins)
                rsqrt_newton(mk, ve, gns[:, 64:80], gns[:, 48:64], gns[:, 80:96])
                ins = ve.engine_nop()
                vc_rs = mk.inc(ins, "Vc")
                mk.wait(ve, "Vc", vc_rs)
                mu_b = gns[:, 32:48].rearrange("p (h o) -> p h o", o=1).broadcast_to([128, 16, 64])
                rs_b = gns[:, 64:80].rearrange("p (h o) -> p h o", o=1).broadcast_to([128, 16, 64])
                ve.tensor_tensor(y3, y3, mu_b, ALU.subtract)
                ve.tensor_tensor(y3, y3, rs_b, ALU.mult)
                ve.tensor_mul(y2[:], y2[:], gbc[:])
                ve.tensor_add(y2[:], y2[:], bbc[:])
                ins = ve.tensor_mul(gA[cc // 2][:], y2[:], gA[cc // 2][:])
                vc_yg = mk.inc(ins, "Vc")
                if dbg is not None and sci == 0 and cc == 1:
                    mk.wait(gp, "Vc", vc_yg)
                    ins = gp.dma_start(dbg["y2"], y2[:])
                    od_y = mk.inc(ins, "Od", 16)
                    mk.wait(sc, "Od", od_y)
                mk.wait(te, "Vc", vc_yg)
                mk.wait(te, "Sg", mk.ctr["Sg"])
                for mb in range(NKB):
                    ins = te.transpose(PJ[1].bitcast(BF16)[:, mb * 128:(mb + 1) * 128],
                                       gA[cc // 2][:, mb * 128:(mb + 1) * 128], id16[:])
                tm_ygt = mk.inc(ins, "Tm")
                mk.wait(sc, "Tm", tm_ygt)
                tc2 = slice((cc // 2) * 128, (cc // 2) * 128 + 128)
                for mb in range(NKB):
                    ins = sc.copy(ygT16[mb][:, tc2], PJ[1].bitcast(BF16)[:, mb * 128:(mb + 1) * 128])
                mk.inc(ins, "Sg")

        if STAGE < 7:
            continue
        sg_ygt = mk.ctr["Sg"]
        mk.wait(te, "Sg", sg_ygt)
        for mb in range(NKB):
            j = mb % 2
            mk.wait(sy, "Tm", mk.ctr["Tm"] if mb < 2 else tm_wp[j])
            wv_ = stream_wcol(4, mb, j)
            pj = PJ[1][:, 0:SC] if False else (PJ[mb % 2][:, 0:SC])
            mk.wait(te, "Wd", wv_)
            mk.wait(te, "Sg", mk.ctr["Sg"])
            mk.wait(te, "Vc", mk.ctr["Vc"])
            for kb in range(NKB):
                ins = te.matmul(pj, wcb[j][:, kb * 128:(kb + 1) * 128], ygT16[kb][:],
                                start=(kb == 0), stop=(kb == NKB - 1))
            tm_wp[j] = mk.inc(ins, "Tm")
            mk.wait(sc, "Tm", tm_wp[j])
            mk.wait(sc, "Od", mk.ctr["Od"])
            ins = sc.copy(rbuf[mb][:], pj)
            mk.inc(ins, "Sg")
        sg_wo = mk.ctr["Sg"]
        mk.wait(ve, "Sg", sg_wo)
        for mb in range(NKB):
            ins = ve.tensor_add(rbuf[mb][:], rbuf[mb][:], x0T[mb][:, ts4])
        mk.inc(ins, "Vc")
        vc_apply = build_ln(mk, ln, [r[:] for r in rbuf], [x[:, ts4] for x in x0T])
        mk.wait(gp, "Vc", vc_apply)
        for i in range(NKB):
            ins = gp.dma_start(x1_dram[i * 128:(i + 1) * 128, ts4], x0T[i][:, ts4])
            mk.inc(ins, "Od", 16)
    mk.wait(sy, "Od", mk.ctr["Od"])

def host_l0_tables():
    i64 = np.eye(64, dtype=np.float32)
    i64t = np.concatenate([i64] * 16, axis=1).astype(bfdt)
    u1 = np.triu(np.ones((64, 64), np.float32), 1)
    ust = np.concatenate([u1] * 8, axis=1).astype(bfdt)
    rmask = np.ones((128, 256), np.float32)
    rmask[:, 0::64] = 0.0
    hs = np.zeros((128, 128), np.float32)
    for kb in range(8):
        hs[0:64, kb * 16 + 2 * kb] = 1.0
        hs[64:128, kb * 16 + 2 * kb + 1] = 1.0
    return dict(i64t=i64t, ust=ust, rmask=rmask, hsel=hs.astype(bfdt),
                idf=np.eye(128, dtype=np.float32), id16=np.eye(128, dtype=bfdt))






def _mk_l0():
    mk = MK()
    x0_d = mk.din("x0", [D, T])
    wd = {}
    shapes = dict(maa_x=(128, 8), maa_w=(128, 8), maa_k=(128, 8), maa_v=(128, 8),
                  maa_r=(128, 8), maa_g=(128, 8), tdc=(128, 8), uc=(128, 8),
                  gamr=(1, 1024), betr=(1, 1024), maaw1=(1024, 160), maaw2=(5, 32, 1024),
                  dw1=(1024, 64), dw2=(64, 1024), W_r=(1024, 1024), W_k=(1024, 1024),
                  W_v=(1024, 1024), W_g=(1024, 1024), W_o=(1024, 1024),
                  hsel=(128, 128), i64t=(64, 1024), ust=(64, 512), rmask=(128, 256),
                  idf=(128, 128), id16=(128, 128))
    bf16keys = {"hsel", "i64t", "ust", "id16"}
    for k, s in shapes.items():
        wd[k] = mk.din(k, list(s), BF16 if k in bf16keys else F32)
    x1_d = mk.dout("x1", [D, T])
    build_l0_phase(mk, x0_d, x1_d, wd)
    return mk


def _mk_ffn():
    mk = MK()
    xh_d = mk.din("xh", [D, T])
    w1_d = mk.din("w1", [D, 4096])
    b1_d = mk.din("b1c", [128, 32])
    w2_d = mk.din("w2", [4096, D])
    b2_d = mk.din("b2c", [128, 8])
    x2_d = mk.dout("x2", [D, T])
    build_ffn_phase(mk, xh_d, x2_d, dict(w1=w1_d, b1c=b1_d, w2=w2_d, b2c=b2_d))
    return mk


def _mk_attn():
    mk = MK()
    x2_d = mk.din("x2", [D, T])
    wd = {}
    shapes = dict(wq=(1024, 1024), wk=(1024, 1024), wv=(1024, 1024), wo=(1024, 1024),
                  ropeC=(128, T), ropeS=(128, T), pswap=(128, 128), hsel=(128, 128),
                  dmask=(128, 2048))
    bf16keys = {"ropeC", "ropeS", "pswap", "hsel", "dmask"}
    for k, s in shapes.items():
        wd[k] = mk.din(k, list(s), BF16 if k in bf16keys else F32)
    x3_d = mk.dout("x3", [D, T])
    build_attn_phase(mk, x2_d, x3_d, wd)
    return mk


def _mk_moe():
    mk = MK()
    x3_d = mk.din("x3", [D, T])
    wd = {}
    shapes = dict(router=(1024, 8), gate=(8, 1024, 1024), up=(8, 1024, 1024),
                  bgc=(128, 64), buc=(128, 64), down=(8, 1024, 1024), dbias=(8, 1024),
                  idf=(128, 128))
    for k, s in shapes.items():
        wd[k] = mk.din(k, list(s), F32)
    out_d = mk.dout("out", [D, T])
    build_moe_phase(mk, x3_d, out_d, wd)
    return mk


def kernel(**inputs):
    z = inputs
    NCORES = 8
    l0w = dict(
        maa_x=host_col(z["L0_attn_time_maa_x"]), maa_w=host_col(z["L0_attn_time_maa_w"]),
        maa_k=host_col(z["L0_attn_time_maa_k"]), maa_v=host_col(z["L0_attn_time_maa_v"]),
        maa_r=host_col(z["L0_attn_time_maa_r"]), maa_g=host_col(z["L0_attn_time_maa_g"]),
        tdc=host_col(z["L0_attn_time_decay"]),
        uc=host_col(np.asarray(z["L0_attn_time_faaaa"]).reshape(-1)),
        gamr=np.ascontiguousarray(np.asarray(z["L0_attn_ln_gamma"], np.float32).reshape(1, -1)),
        betr=np.ascontiguousarray(np.asarray(z["L0_attn_ln_beta"], np.float32).reshape(1, -1)),
        maaw1=np.ascontiguousarray(z["L0_attn_time_maa_w1"], dtype=np.float32),
        maaw2=np.ascontiguousarray(z["L0_attn_time_maa_w2"], dtype=np.float32),
        dw1=np.ascontiguousarray(z["L0_attn_time_decay_w1"], dtype=np.float32),
        dw2=np.ascontiguousarray(z["L0_attn_time_decay_w2"], dtype=np.float32),
        W_r=np.ascontiguousarray(z["L0_attn_W_r"], dtype=np.float32),
        W_k=np.ascontiguousarray(z["L0_attn_W_k"], dtype=np.float32),
        W_v=np.ascontiguousarray(z["L0_attn_W_v"], dtype=np.float32),
        W_g=np.ascontiguousarray(z["L0_attn_W_g"], dtype=np.float32),
        W_o=np.ascontiguousarray(z["L0_attn_W_o"], dtype=np.float32),
    )
    l0w.update(host_l0_tables())
    ffnw = dict(w1=np.ascontiguousarray(z["L0_ffn_w1"], dtype=np.float32),
                b1c=host_col(np.asarray(z["L0_ffn_b1"])),
                w2=np.ascontiguousarray(z["L0_ffn_w2"], dtype=np.float32),
                b2c=host_col(np.asarray(z["L0_ffn_b2"])))
    atw = host_attn_weights(np.asarray(z["L1_attn_W_qkv"], np.float32),
                            np.asarray(z["L1_attn_W_o"], np.float32))
    atw.update(host_attn_tables())
    mow = host_moe_weights(np.asarray(z["L1_ffn_router_weight"], np.float32),
                           np.asarray(z["L1_ffn_gate_up_proj"], np.float32),
                           np.asarray(z["L1_ffn_gate_up_proj_bias"], np.float32),
                           np.asarray(z["L1_ffn_down_proj"], np.float32),
                           np.asarray(z["L1_ffn_down_proj_bias"], np.float32))
    x0 = np.asarray(z["L0_attn_x"], np.float32)
    xt = [np.ascontiguousarray(x0[b].T) for b in range(NCORES)]

    mk = _mk_l0()
    with mk.stack:
        pass
    in_maps = [dict(l0w, x0=xt[b]) for b in range(NCORES)]
    res = run_bass_kernel_spmd(mk.nc, in_maps, core_ids=list(range(NCORES)))
    x1 = [np.ascontiguousarray(res.results[b]["x1"]) for b in range(NCORES)]

    mk = _mk_ffn()
    with mk.stack:
        pass
    in_maps = [dict(ffnw, xh=x1[b]) for b in range(NCORES)]
    res = run_bass_kernel_spmd(mk.nc, in_maps, core_ids=list(range(NCORES)))
    x2 = [np.ascontiguousarray(res.results[b]["x2"]) for b in range(NCORES)]

    mk = _mk_attn()
    with mk.stack:
        pass
    in_maps = [dict(atw, x2=x2[b]) for b in range(NCORES)]
    res = run_bass_kernel_spmd(mk.nc, in_maps, core_ids=list(range(NCORES)))
    x3 = [np.ascontiguousarray(res.results[b]["x3"]) for b in range(NCORES)]

    mk = _mk_moe()
    with mk.stack:
        pass
    in_maps = [dict(mow, x3=x3[b]) for b in range(NCORES)]
    res = run_bass_kernel_spmd(mk.nc, in_maps, core_ids=list(range(NCORES)))
    out = np.stack([np.ascontiguousarray(res.results[b]["out"].T) for b in range(NCORES)])
    return out


# version 03 (final)
# speedup vs baseline: 1.0017x; 1.0017x over previous
import numpy as np
import ml_dtypes
import contextlib
from collections import defaultdict
import concourse.bass as bass
import concourse.mybir as mybir
from concourse.bass_utils import run_bass_kernel_spmd

F32 = mybir.dt.float32
BF16 = mybir.dt.bfloat16
I32 = mybir.dt.int32
AF = mybir.ActivationFunctionType
ALU = mybir.AluOpType
AX = mybir.AxisListType.X
bfdt = ml_dtypes.bfloat16

D = 1024
T = 2048
NKB = 8
EPS_LN = 1e-5
EPS_GN = 6.4e-4


class MK:
    def __init__(self):
        self.nc = bass.Bass("TRN2", target_bir_lowering=False)
        self.stack = contextlib.ExitStack()
        self.semh = {}
        self.ctr = defaultdict(int)
        self.waited = {}

    def sem(self, name):
        if name not in self.semh:
            self.semh[name] = self.stack.enter_context(self.nc.semaphore(name))
            self.ctr[name] = 0
        return self.semh[name]

    def inc(self, instr, name, k=1):
        h = self.sem(name)
        self.ctr[name] += k
        instr.then_inc(h, k)
        return self.ctr[name]

    def wait(self, eng, name, val=None):
        h = self.sem(name)
        if val is None:
            val = self.ctr[name]
        assert val <= self.ctr[name], f"wait target {val} > emitted {self.ctr[name]} for {name}"
        key = (id(eng), name)
        if self.waited.get(key, -1) >= val:
            return
        eng.wait_ge(h, val)
        self.waited[key] = val

    def join(self, *engs):
        for e_ in engs:
            for s_ in ("Tm", "Vc", "Sg", "Gc", "Wd", "Xd", "Ad", "Od", "Wc"):
                self.wait(e_, s_)

    def sb(self, name, shape, dtype):
        return self.stack.enter_context(self.nc.sbuf_tensor(name, shape, dtype))

    def ps(self, name, shape, dtype=F32):
        return self.stack.enter_context(self.nc.psum_tensor(name, shape, dtype))

    def din(self, name, shape, dtype=F32):
        return self.nc.dram_tensor(name, list(shape), dtype, kind="ExternalInput").ap()

    def dout(self, name, shape, dtype=F32):
        return self.nc.dram_tensor(name, list(shape), dtype, kind="ExternalOutput").ap()

    def dtmp(self, name, shape, dtype=F32):
        return self.nc.dram_tensor(name, list(shape), dtype, kind="Internal").ap()


def vsync(mk, ve, ins):
    v = mk.inc(ins, "Vc")
    mk.wait(ve, "Vc", v)


def rsqrt_newton(mk, ve, out, x, t2, iters=4):
    yi = out.bitcast(I32)
    ins = ve.tensor_single_scalar(yi, x.bitcast(I32), 1, ALU.arith_shift_right)
    vsync(mk, ve, ins)
    ins = ve.tensor_scalar(yi, yi, -1, 0x5F3759DF, ALU.mult, ALU.add)
    vsync(mk, ve, ins)
    for _ in range(iters):
        ins = ve.tensor_mul(t2, out, out)
        vsync(mk, ve, ins)
        ins = ve.tensor_mul(t2, t2, x)
        vsync(mk, ve, ins)
        ins = ve.tensor_scalar(t2, t2, -0.5, 1.5, ALU.mult, ALU.add)
        vsync(mk, ve, ins)
        last = ve.tensor_mul(out, out, t2)
        if _ < iters - 1:
            vsync(mk, ve, last)
    return last



def alloc_ln_scratch(mk, TC, pre="ln", pst=None, pbcb=None, r16s=None):
    if r16s is None:
        r16b = mk.sb(pre + "_r16", [128, NKB * TC], BF16)
        r16s = [r16b[:, i * TC:(i + 1) * TC] for i in range(NKB)]
    d = dict(
        TC=TC,
        r16s=r16s,
        ones16=mk.sb(pre + "_ones16", [128, 1], BF16),
        onesf=mk.sb(pre + "_onesf", [1, 128], F32),
        statrow=mk.sb(pre + "_statrow", [1, 5 * TC], F32),
        pst=pst if pst is not None else mk.ps(pre + "_pst", [1, 2 * TC]),
        pbcb=pbcb if pbcb is not None else mk.ps(pre + "_pbc", [128, 2 * TC]),
    )
    d["pbc"] = [d["pbcb"][:, 0:TC], d["pbcb"][:, TC:2 * TC]]
    sr = d["statrow"]
    for k, i in [("sm", 0), ("svar", 1), ("st1", 2), ("srs", 3), ("smrs", 4)]:
        d[k] = sr[:, i * TC:(i + 1) * TC]
    return d


def init_ln_scratch(mk, ln):
    ve = mk.nc.vector
    ve.memset(ln["ones16"][:], 1.0)
    ins = ve.memset(ln["onesf"][:], 1.0)
    mk.inc(ins, "Vc")


def build_ln(mk, ln, rbuf, xout, x16out=None, dbg=None):
    nc = mk.nc
    te, ve, sc = nc.tensor, nc.vector, nc.scalar
    TC = ln["TC"]
    tm_entry = mk.ctr["Tm"]
    mk.wait(ve, "Tm", tm_entry)
    for i in range(NKB):
        vins = ve.tensor_copy(ln["r16s"][i], rbuf[i])
    vc_all = mk.inc(vins, "Vc")
    mk.wait(te, "Vc", vc_all)
    for i in range(NKB):
        ins = te.matmul(ln["pst"][0:1, 0:TC], ln["ones16"][:], ln["r16s"][i],
                        start=(i == 0), stop=(i == NKB - 1))
    tm_a = mk.inc(ins, "Tm")
    mk.wait(sc, "Tm", tm_a)
    for i in range(NKB):
        sins = sc.activation(ln["r16s"][i], rbuf[i], AF.Square)
    sg_all = mk.inc(sins, "Sg")
    mk.wait(te, "Sg", sg_all)
    for i in range(NKB):
        ins = te.matmul(ln["pst"][0:1, TC:2 * TC], ln["ones16"][:], ln["r16s"][i],
                        start=(i == 0), stop=(i == NKB - 1))
    tm_st = mk.inc(ins, "Tm")
    mk.wait(ve, "Tm", tm_st)
    ins = ve.tensor_scalar_mul(ln["sm"], ln["pst"][0:1, 0:TC], 1.0 / D)
    ve.tensor_scalar_mul(ln["svar"], ln["pst"][0:1, TC:2 * TC], 1.0 / D)
    vsync(mk, ve, ins)
    ins = ve.tensor_mul(ln["st1"], ln["sm"], ln["sm"])
    vsync(mk, ve, ins)
    ins = ve.tensor_scalar(ln["svar"], ln["svar"], 1.0, EPS_LN, ALU.mult, ALU.add)
    vsync(mk, ve, ins)
    ins = ve.tensor_sub(ln["svar"], ln["svar"], ln["st1"])
    vsync(mk, ve, ins)
    rsqrt_newton(mk, ve, ln["srs"], ln["svar"], ln["st1"])
    ins = ve.tensor_mul(ln["smrs"], ln["sm"], ln["srs"])
    vc_rs = mk.inc(ins, "Vc")
    mk.wait(ve, "Vc", vc_rs)
    mk.wait(te, "Vc", vc_rs)
    te.matmul(ln["pbc"][0], ln["onesf"][:], ln["srs"], start=True, stop=True)
    ins = te.matmul(ln["pbc"][1], ln["onesf"][:], ln["smrs"], start=True, stop=True)
    tm_bc = mk.inc(ins, "Tm")
    if dbg is not None:
        gp = nc.gpsimd
        mk.wait(gp, "Vc", vc_rs)
        ins = gp.dma_start(dbg["stat"], ln["statrow"][:])
        mk.inc(ins, "Od", 16)
    mk.wait(ve, "Tm", tm_bc)
    for i in range(NKB):
        ve.tensor_mul(xout[i], rbuf[i], ln["pbc"][0])
        ins = ve.tensor_sub(xout[i], xout[i], ln["pbc"][1])
        if x16out is not None:
            ins = ve.tensor_copy(x16out[i], xout[i])
    return mk.inc(ins, "Vc")



def build_ffn_phase(mk, xh_dram, x2_dram, wd, dbg=None):
    nc = mk.nc
    sy, te, ve, sc, gp = nc.sync, nc.tensor, nc.vector, nc.scalar, nc.gpsimd
    TC = 256
    NTC = T // TC
    HB = 32

    x16 = [mk.sb(f"ffn_x16_{i}", [128, T], BF16) for i in range(NKB)]
    w1b = [mk.sb(f"ffn_w1_{i}", [128, 4096], BF16) for i in range(NKB)]
    w2b = [mk.sb(f"ffn_w2_{i}", [128, 1024], BF16) for i in range(HB)]
    wst = [mk.sb("ffn_wst_0", [128, 1024], F32)]
    h16 = [mk.sb(f"ffn_h16_{i}", [128, TC], BF16) for i in range(HB)]
    b1c = mk.sb("ffn_b1c", [128, 32], F32)
    b2c = mk.sb("ffn_b2c", [128, 8], F32)
    rbuf = [mk.sb(f"ffn_r_{i}", [128, TC], F32) for i in range(NKB)]
    xhres = [mk.sb(f"ffn_xh_{i}", [128, TC], F32) for i in range(NKB)]
    ln = alloc_ln_scratch(mk, TC)

    pout = [mk.ps(f"ffn_pout_{i}", [128, TC])[:, :] for i in range(4)]
    ph = [mk.ps(f"ffn_ph_{i}", [128, TC])[:, :] for i in range(2)]

    ins = sy.dma_start(b1c[:], wd["b1c"]); mk.inc(ins, "Wd", 16)
    ins = sy.dma_start(b2c[:], wd["b2c"]); mk.inc(ins, "Wd", 16)
    init_ln_scratch(mk, ln)

    loads = []
    for i in range(NKB):
        for q in range(2):
            loads.append((xh_dram[i * 128:(i + 1) * 128, q * 1024:(q + 1) * 1024],
                          x16[i][:, q * 1024:(q + 1) * 1024], 1024))
    for i in range(NKB):
        for q in range(4):
            loads.append((wd["w1"][i * 128:(i + 1) * 128, q * 1024:(q + 1) * 1024],
                          w1b[i][:, q * 1024:(q + 1) * 1024], 1024))
    for i in range(HB):
        loads.append((wd["w2"][i * 128:(i + 1) * 128, :], w2b[i][:], 1024))
    cast_done = []
    for li, (dsl, dst, ncols) in enumerate(loads):
        st = wst[0]
        if li >= 1:
            mk.wait(sy, "Wc", cast_done[li - 1])
        ins = sy.dma_start(st[:, 0:ncols], dsl)
        wd_val = mk.inc(ins, "Wd", 16)
        eng = ve if li % 2 == 0 else sc
        mk.wait(eng, "Wd", wd_val)
        if eng is ve:
            ins = eng.tensor_copy(dst, st[:, 0:ncols])
        else:
            ins = eng.copy(dst, st[:, 0:ncols])
        cast_done.append(mk.inc(ins, "Wc"))
    wc_all = mk.ctr["Wc"]

    sg_h = [0, 0]
    tm_h16free = [0, 0]
    vc_res = [0]
    sg_pout = 0
    vc_apply = 0

    for tc in range(NTC):
        t0 = tc * TC
        tsl = slice(t0, t0 + TC)
        if tc >= 1:
            mk.wait(gp, "Vc", vc_res[0])
        for i in range(NKB):
            ins = gp.dma_start(xhres[i][:], xh_dram[i * 128:(i + 1) * 128, tsl])
            xd_res = mk.inc(ins, "Xd", 16)

        mk.wait(te, "Wc", wc_all)
        for hb in range(HB):
            j = hb % 2
            hp = ph[j]
            if sg_h[j]:
                mk.wait(te, "Sg", sg_h[j])
            for kb in range(NKB):
                ins = te.matmul(hp, w1b[kb][:, hb * 128:(hb + 1) * 128],
                                x16[kb][:, tsl], start=(kb == 0), stop=(kb == NKB - 1))
            tm_hmm = mk.inc(ins, "Tm")
            mk.wait(sc, "Tm", tm_hmm)
            ins = sc.activation(h16[hb][:], hp, AF.Gelu,
                                bias=b1c[:, hb:hb + 1], scale=1.0)
            sg_h[j] = mk.inc(ins, "Sg")
            if dbg is not None and tc == 0 and hb == 0:
                mk.wait(gp, "Sg", sg_h[j])
                ins = gp.dma_start(dbg["h16"], h16[hb][:])
                mk.inc(ins, "Od", 16)
        sg_gelus = mk.ctr["Sg"]
        mk.wait(te, "Sg", sg_gelus)
        if sg_pout:
            mk.wait(te, "Sg", sg_pout)
        if vc_apply:
            mk.wait(sc, "Od", mk.ctr["Od"])
        for half in range(2):
            for hb in range(HB):
                for m4 in range(4):
                    mb = half * 4 + m4
                    ins = te.matmul(pout[m4], w2b[hb][:, mb * 128:(mb + 1) * 128],
                                    h16[hb][:], start=(hb == 0), stop=(hb == HB - 1))
            tm_out2 = mk.inc(ins, "Tm")
            mk.wait(sc, "Tm", tm_out2)
            for m4 in range(4):
                mb = half * 4 + m4
                ins = sc.activation(rbuf[mb][:], pout[m4], AF.Identity, bias=b2c[:, mb:mb + 1])
            sg_pout = mk.inc(ins, "Sg")
            if half == 0:
                mk.wait(te, "Sg", sg_pout)
        mk.wait(ve, "Sg", sg_pout)
        mk.wait(ve, "Xd", xd_res)
        for mb in range(NKB):
            ins = ve.tensor_add(rbuf[mb][:], rbuf[mb][:], xhres[mb][:])
        vc_res[0] = mk.inc(ins, "Vc")
        if dbg is not None and tc == 0:
            mk.wait(gp, "Vc", vc_res[0])
            for mb in range(NKB):
                ins = gp.dma_start(dbg["r"][mb * 128:(mb + 1) * 128, :], rbuf[mb][:])
                mk.inc(ins, "Od", 16)

        vc_apply = build_ln(mk, ln, [r[:] for r in rbuf], [r[:] for r in rbuf],
                            dbg=(dbg if tc == 0 else None))

        mk.wait(gp, "Vc", vc_apply)
        for i in range(NKB):
            ins = gp.dma_start(x2_dram[i * 128:(i + 1) * 128, tsl], rbuf[i][:])
            mk.inc(ins, "Od", 16)

    mk.wait(sy, "Od", mk.ctr["Od"])



def host_col(v, nb=None):
    v = np.asarray(v, np.float32)
    n = v.shape[0]
    return np.ascontiguousarray(v.reshape(n // 128, 128).T)




def build_attn_phase(mk, x2_dram, x3_dram, wd, dbg=None):
    nc = mk.nc
    sy, te, ve, sc, gp = nc.sync, nc.tensor, nc.vector, nc.scalar, nc.gpsimd
    NH = 16

    qT = [mk.sb(f"at_q_{i}", [128, T], BF16) for i in range(NKB)]
    kT = [mk.sb(f"at_k_{i}", [128, T], BF16) for i in range(NKB)]
    V16 = [mk.sb(f"at_v_{i}", [128, 1040], BF16) for i in range(16)]
    wXb = [mk.sb(f"at_w_{i}", [128, D], BF16) for i in range(NKB)]
    wst = mk.sb("at_wst", [128, 2048], F32)
    ropeC = mk.sb("at_ropeC", [128, T], BF16)
    ropeS = mk.sb("at_ropeS", [128, T], BF16)
    pswap = mk.sb("at_pswap", [128, 128], BF16)
    hsel = mk.sb("at_hsel", [128, 16 * 8], BF16)
    dmask = mk.sb("at_dmask", [128, 512 * 4], BF16)
    pre16 = mk.sb("at_pre16", [128, 512], BF16)
    sw16 = mk.sb("at_sw16", [128, 512], BF16)
    t116 = mk.sb("at_t116", [128, 512], BF16)
    coldr = mk.dtmp("at_coldr", [16], F32)
    xstack = contextlib.ExitStack()
    x16 = [xstack.enter_context(mk.nc.sbuf_tensor(f"at_x16_{i}", [128, T], BF16))
           for i in range(NKB)]

    P0 = mk.ps("at_P0", [128, 512])[:, :]
    P1 = mk.ps("at_P1", [128, 512])[:, :]
    P2 = mk.ps("at_P2", [128, 512])[:, :]
    P3 = mk.ps("at_P3", [128, 512])[:, :]
    pv = mk.ps("at_pv", [65, 512])[:, :]

    for name, dst in [("ropeC", ropeC), ("ropeS", ropeS), ("pswap", pswap),
                      ("hsel", hsel), ("dmask", dmask)]:
        ins = sy.dma_start(dst[:], wd[name])
        mk.inc(ins, "Wd", 16)
    for tb in range(16):
        ap = V16[tb][:].rearrange("p (h c) -> p h c", c=65)[:, :, 64:65]
        ins = ve.memset(ap, 1.0)
    vc_init = mk.inc(ins, "Vc")

    for i in range(NKB):
        ins = gp.dma_start(wst[:, 0:T], x2_dram[i * 128:(i + 1) * 128, :])
        xd = mk.inc(ins, "Xd", 16)
        mk.wait(ve, "Xd", xd)
        ins = ve.tensor_copy(x16[i][:], wst[:, 0:T])
        vc_x = mk.inc(ins, "Vc")
        mk.wait(gp, "Vc", vc_x)
    vc_x16 = mk.ctr["Vc"]

    def load_w(wdram):
        mk.wait(sy, "Tm", mk.ctr["Tm"])
        mk.wait(sy, "Vc", mk.ctr["Vc"])
        mk.wait(sy, "Sg", mk.ctr["Sg"])
        for i in range(NKB):
            ins = sy.dma_start(wst[:, 0:1024], wdram[i * 128:(i + 1) * 128, :])
            wv_ = mk.inc(ins, "Wd", 16)
            eng = sc if i % 2 == 0 else ve
            mk.wait(eng, "Tm", mk.ctr["Tm"])
            mk.wait(eng, "Wd", wv_)
            if eng is sc:
                ins = eng.copy(wXb[i][:], wst[:, 0:1024])
                mk.inc(ins, "Sg")
                mk.wait(sy, "Sg", mk.ctr["Sg"])
            else:
                ins = eng.tensor_copy(wXb[i][:], wst[:, 0:1024])
                mk.inc(ins, "Vc")
                mk.wait(sy, "Vc", mk.ctr["Vc"])
        return mk.ctr["Sg"], mk.ctr["Vc"]

    for wname, dstT in [("wq", qT), ("wk", kT)]:
        sgw, vcw = load_w(wd[wname])
        mk.wait(te, "Sg", sgw)
        mk.wait(te, "Vc", max(vcw, vc_x16))
        for mb in range(NKB):
            for tb in range(4):
                ts4 = slice(tb * 512, (tb + 1) * 512)
                pj = P0 if (mb * 4 + tb) % 2 == 0 else P1
                mk.wait(te, "Sg", mk.ctr["Sg"])
                mk.wait(te, "Vc", mk.ctr["Vc"])
                mk.wait(te, "Gc", mk.ctr.get("Gc", 0))
                for kb in range(NKB):
                    ins = te.matmul(pj, wXb[kb][:, mb * 128:(mb + 1) * 128],
                                    x16[kb][:, ts4], start=(kb == 0), stop=(kb == NKB - 1))
                tmv = mk.inc(ins, "Tm")
                mk.wait(sc, "Tm", tmv)
                mk.wait(sc, "Vc", mk.ctr["Vc"])
                ins = sc.copy(pre16[:], pj)
                sgp = mk.inc(ins, "Sg")
                mk.wait(te, "Sg", sgp)
                ins = te.matmul(P2, pswap[:], pre16[:], start=True, stop=True)
                tms = mk.inc(ins, "Tm")
                mk.wait(sc, "Tm", tms)
                mk.wait(sc, "Gc", mk.ctr.get("Gc", 0))
                ins = sc.copy(sw16[:], P2)
                sgs = mk.inc(ins, "Sg")
                mk.wait(ve, "Sg", sgp)
                ins = ve.tensor_mul(t116[:], pre16[:], ropeC[:, ts4])
                mk.inc(ins, "Vc")
                mk.wait(gp, "Sg", sgs)
                ins = gp.tensor_mul(sw16[:], sw16[:], ropeS[:, ts4])
                gc1 = mk.inc(ins, "Gc")
                mk.wait(ve, "Gc", gc1)
                ins = ve.tensor_add(dstT[mb][:, ts4], t116[:], sw16[:])
                mk.inc(ins, "Vc")

    sgw, vcw = load_w(wd["wv"])
    mk.wait(te, "Sg", sgw)
    mk.wait(te, "Vc", max(vcw, vc_init))
    for tb in range(16):
        for half in range(2):
            pj = P0 if (tb * 2 + half) % 2 == 0 else P1
            mk.wait(te, "Vc", mk.ctr["Vc"])
            for kb in range(NKB):
                ins = te.matmul(pj, x16[kb][:, tb * 128:(tb + 1) * 128],
                                wXb[kb][:, half * 512:(half + 1) * 512],
                                start=(kb == 0), stop=(kb == NKB - 1))
            tmv = mk.inc(ins, "Tm")
            mk.wait(ve, "Tm", tmv)
            src = pj.rearrange("p (h c) -> p h c", c=64)
            dst = V16[tb][:, half * 520:(half + 1) * 520].rearrange("p (h c) -> p h c", c=65)[:, :, 0:64]
            ins = ve.tensor_copy(dst, src)
            mk.inc(ins, "Vc")

    if dbg is not None:
        mk.wait(gp, "Vc", mk.ctr["Vc"])
        ins = gp.dma_start(dbg["q"], qT[0][:])
        mk.inc(ins, "Od", 16)
        ins = gp.dma_start(dbg["k"], kT[0][:])
        mk.inc(ins, "Od", 16)
        ins = gp.dma_start(dbg["v"], V16[0][:])
        mk.inc(ins, "Od", 16)
    xstack.close()
    sq16 = mk.sb("at_sq16", [128, 4096], BF16)
    PT = [mk.sb(f"at_pt_{i}", [128, 512], BF16) for i in range(16)]
    o16 = mk.sb("at_o16", [128, 512], BF16)
    attnT = [mk.sb(f"at_attnT_{i}", [128, 512], BF16) for i in range(NKB)]
    rrow = mk.sb("at_rrow", [128, 512], F32)
    nstat = mk.sb("at_nstat", [128, 24], F32)
    onesall = mk.sb("at_onesall", [128, 128], F32)
    rowtmp = mk.sb("at_rowtmp", [1, 16], F32)
    rbuf = [mk.sb(f"at_r_{i}", [128, 256], F32) for i in range(NKB)]
    xres = [mk.sb(f"at_xh_{i}", [128, 256], F32) for i in range(NKB)]
    ln = alloc_ln_scratch(mk, 256, "atln")
    init_ln_scratch(mk, ln)
    ins = ve.memset(onesall[:], 1.0)
    mk.inc(ins, "Vc")
    for idx, srcT in [(0, qT), (1, kT)]:
        for tb in range(4):
            ts4 = slice(tb * 512, (tb + 1) * 512)
            mk.wait(sc, "Tm", mk.ctr["Tm"])
            for i in range(NKB):
                ins = sc.activation(sq16[:, i * 512:(i + 1) * 512], srcT[i][:, ts4], AF.Square)
            sgq = mk.inc(ins, "Sg")
            mk.wait(te, "Sg", sgq)
            for i in range(NKB):
                ins = te.matmul(P3[0:16, :], hsel[:, i * 16:(i + 1) * 16],
                                sq16[:, i * 512:(i + 1) * 512],
                                start=(i == 0), stop=(i == NKB - 1))
            mk.inc(ins, "Tm")
            mk.wait(ve, "Tm", mk.ctr["Tm"])
            if idx == 0 and tb == 0:
                ins = ve.reduce_max(nstat[0:16, 0:1], P3[0:16, :], axis=AX)
            elif idx == 1 and tb == 0:
                ins = ve.reduce_max(nstat[0:16, 1:2], P3[0:16, :], axis=AX)
            else:
                ve.reduce_max(nstat[0:16, 4:5], P3[0:16, :], axis=AX)
                ins = ve.tensor_max(nstat[0:16, idx:idx + 1], nstat[0:16, idx:idx + 1],
                                    nstat[0:16, 4:5])
            vcq = mk.inc(ins, "Vc")
            mk.wait(te, "Vc", vcq)
    ve.tensor_mul(nstat[0:16, 2:3], nstat[0:16, 0:1], nstat[0:16, 1:2])
    ve.tensor_scalar_add(nstat[0:16, 2:3], nstat[0:16, 2:3], 1e-12)
    rsqrt_newton(mk, ve, nstat[0:16, 3:4], nstat[0:16, 2:3], nstat[0:16, 4:5])
    ve.tensor_mul(nstat[0:16, 5:6], nstat[0:16, 2:3], nstat[0:16, 3:4])
    ins = ve.tensor_scalar(nstat[0:16, 5:6], nstat[0:16, 5:6], -0.125, 70.0,
                           ALU.mult, ALU.add)
    vcm = mk.inc(ins, "Vc")
    mk.wait(gp, "Vc", vcm)
    ins = gp.dma_start(coldr.rearrange("(a b) -> a b", b=1), nstat[0:16, 5:6])
    gd = mk.inc(ins, "Gd", 16)
    mk.wait(gp, "Gd", gd)
    ins = gp.dma_start(rowtmp[:], coldr.rearrange("(a b) -> a b", a=1))
    gd = mk.inc(ins, "Gd", 16)
    mk.wait(te, "Gd", gd)
    mk.wait(te, "Vc", mk.ctr["Vc"])
    ins = te.matmul(P2[:, 0:16], ln["onesf"][:], rowtmp[:], start=True, stop=True)
    tmb = mk.inc(ins, "Tm")
    mk.wait(ve, "Tm", tmb)
    ins = ve.tensor_copy(nstat[:, 6:22], P2[:, 0:16])
    vc_mh = mk.inc(ins, "Vc")

    if dbg is not None:
        mk.wait(gp, "Vc", vc_mh)
        ins = gp.dma_start(dbg["nstat"], nstat[:])
        mk.inc(ins, "Od", 16)
    sg_wo, vc_wo = load_w(wd["wo"])

    sg_p = {0: 0, 1: 0}
    for qb in range(4):
        nkb = (qb + 1) * 4
        for h in range(NH):
            mt = h // 2
            hr = slice(64 * (h % 2), 64 * (h % 2) + 64)
            for kb in range(nkb):
                j = kb % 2
                pj = P0 if j == 0 else P1
                if sg_p[j]:
                    mk.wait(te, "Sg", sg_p[j])
                ins = te.matmul(pj, kT[mt][hr, kb * 128:(kb + 1) * 128],
                                qT[mt][hr, qb * 512:(qb + 1) * 512], start=True, stop=True)
                tms = mk.inc(ins, "Tm")
                mk.wait(sc, "Tm", tms)
                mk.wait(sc, "Vc", vc_mh)
                mk.wait(sc, "Gc", mk.ctr.get("Gc", 0))
                ins = sc.activation(PT[kb][:], pj, AF.Exp,
                                    bias=nstat[:, 6 + h:7 + h], scale=0.125)
                sg_p[j] = mk.inc(ins, "Sg")
            sg_exp = mk.ctr["Sg"]
            mk.wait(gp, "Sg", sg_exp)
            for dkb in range(4):
                kb = qb * 4 + dkb
                ins = gp.tensor_mul(PT[kb][:], PT[kb][:], dmask[:, dkb * 512:(dkb + 1) * 512])
            gc_mask = mk.inc(ins, "Gc")
            mk.wait(te, "Sg", sg_exp)
            mk.wait(te, "Gc", gc_mask)
            mk.wait(te, "Vc", mk.ctr["Vc"])
            mk.wait(te, "Sg", mk.ctr["Sg"])
            for kb in range(nkb):
                ins = te.matmul(pv, V16[kb][:, 65 * h:65 * h + 65], PT[kb][:],
                                start=(kb == 0), stop=(kb == nkb - 1))
            tm_pv = mk.inc(ins, "Tm")
            mk.wait(sc, "Tm", tm_pv)
            mk.wait(sc, "Gd", mk.ctr.get("Gd", 0))
            mk.wait(sc, "Vc", mk.ctr["Vc"])
            ins = sc.copy(o16[0:64, :], pv[0:64, :])
            sg_o = mk.inc(ins, "Sg")
            mk.wait(ve, "Tm", tm_pv)
            ins = ve.reciprocal(rrow[64:65, :], pv[64:65, :])
            vc_rr = mk.inc(ins, "Vc")
            mk.wait(te, "Vc", vc_rr)
            ins = te.matmul(P2[0:64, :], onesall[64:65, 0:64], rrow[64:65, :],
                            start=True, stop=True)
            tm_nb = mk.inc(ins, "Tm")
            mk.wait(ve, "Tm", tm_nb)
            mk.wait(ve, "Sg", sg_o)
            if h % 2 == 0:
                ins = ve.tensor_tensor(attnT[mt][0:64, :], o16[0:64, :], P2[0:64, :], ALU.mult)
                mk.inc(ins, "Vc")
            else:
                ins = ve.tensor_tensor(o16[0:64, :], o16[0:64, :], P2[0:64, :], ALU.mult)
                vc_n = mk.inc(ins, "Vc")
                mk.wait(gp, "Vc", vc_n)
                ins = gp.dma_start(attnT[mt][64:128, 0:512], o16[0:64, :])
                mk.inc(ins, "Gd", 16)
        if dbg is not None and qb == 0:
            mk.wait(gp, "Vc", mk.ctr["Vc"])
            mk.wait(gp, "Gd", mk.ctr["Gd"])
            ins = gp.dma_start(dbg["attn"], attnT[0][:])
            mk.inc(ins, "Od", 16)
            ins = gp.dma_start(dbg["pt"], PT[0][:])
            mk.inc(ins, "Od", 16)
        mk.wait(te, "Sg", sg_wo)
        mk.wait(te, "Vc", mk.ctr["Vc"])
        mk.wait(te, "Gd", mk.ctr["Gd"])
        for half in range(2):
            hs = slice(qb * 512 + half * 256, qb * 512 + half * 256 + 256)
            mk.wait(gp, "Vc", mk.ctr["Vc"])
            for i in range(NKB):
                ins = gp.dma_start(xres[i][:], x2_dram[i * 128:(i + 1) * 128, hs])
                xd_r = mk.inc(ins, "Xd", 16)
            for mb in range(NKB):
                pj = (P0 if mb % 2 == 0 else P1)[:, 0:256]
                mk.wait(te, "Sg", mk.ctr["Sg"])
                for kb in range(NKB):
                    ins = te.matmul(pj, wXb[kb][:, mb * 128:(mb + 1) * 128],
                                    attnT[kb][:, half * 256:half * 256 + 256],
                                    start=(kb == 0), stop=(kb == NKB - 1))
                tmo = mk.inc(ins, "Tm")
                mk.wait(sc, "Tm", tmo)
                mk.wait(sc, "Od", mk.ctr.get("Od", 0))
                ins = sc.copy(rbuf[mb][:], pj)
                mk.inc(ins, "Sg")
            sg_wo_e = mk.ctr["Sg"]
            mk.wait(ve, "Sg", sg_wo_e)
            mk.wait(ve, "Xd", xd_r)
            for mb in range(NKB):
                ins = ve.tensor_add(rbuf[mb][:], rbuf[mb][:], xres[mb][:])
            mk.inc(ins, "Vc")
            vc_apply = build_ln(mk, ln, [r[:] for r in rbuf], [r[:] for r in rbuf])
            mk.wait(gp, "Vc", vc_apply)
            for i in range(NKB):
                ins = gp.dma_start(x3_dram[i * 128:(i + 1) * 128, hs], rbuf[i][:])
                mk.inc(ins, "Od", 16)
    mk.wait(sy, "Od", mk.ctr["Od"])


def host_attn_tables():
    pos = np.arange(T, dtype=np.float64)
    inv = 10000.0 ** (-np.arange(0, 64, 2, dtype=np.float64) / 64)
    ang = pos[None, :] * inv[:, None]
    c = np.cos(ang); s = np.sin(ang)
    C = np.concatenate([c, c, c, c], 0).astype(bfdt)
    S = np.concatenate([-s, s, -s, s], 0).astype(bfdt)
    P = np.zeros((128, 128), np.float32)
    for j in range(128):
        src = j + 32 if (j % 64) < 32 else j - 32
        P[src, j] = 1.0
    hsel = np.zeros((8, 128, 16), np.float32)
    for m in range(8):
        hsel[m, 0:64, 2 * m] = 1.0
        hsel[m, 64:128, 2 * m + 1] = 1.0
    hs = np.concatenate([hsel[m] for m in range(8)], axis=1)
    dm = np.zeros((4, 128, 512), np.float32)
    for dkb in range(4):
        for ss in range(128):
            dm[dkb, ss, dkb * 128 + ss:] = 1.0
    dmc = np.concatenate([dm[i] for i in range(4)], axis=1)
    return dict(ropeC=np.ascontiguousarray(C), ropeS=np.ascontiguousarray(S),
                pswap=P.astype(bfdt), hsel=hs.astype(bfdt), dmask=dmc.astype(bfdt))


def host_attn_weights(W_qkv, W_o):
    perm = np.zeros(64, np.int64)
    perm[0:32] = np.arange(0, 64, 2)
    perm[32:64] = np.arange(1, 64, 2)
    full_perm = np.concatenate([h * 64 + perm for h in range(16)])
    wq = np.ascontiguousarray(W_qkv[0:1024][full_perm].T)
    wk = np.ascontiguousarray(W_qkv[1024:2048][full_perm].T)
    wv = np.ascontiguousarray(W_qkv[2048:3072].T)
    wo = np.ascontiguousarray(W_o.T)
    return dict(wq=wq, wk=wk, wv=wv, wo=wo)



def build_moe_phase(mk, x3_dram, out_dram, wd, dbg=None):
    nc = mk.nc
    sy, te, ve, sc, gp = nc.sync, nc.tensor, nc.vector, nc.scalar, nc.gpsimd
    NE = 8
    AL = 1.702

    x16 = [mk.sb(f"mo_x16_{i}", [128, T], BF16) for i in range(NKB)]
    wg16 = [mk.sb(f"mo_wg_{i}", [128, D], BF16) for i in range(NKB)]
    wu16 = [mk.sb(f"mo_wu_{i}", [128, D], BF16) for i in range(NKB)]
    wdn8 = [mk.sb(f"mo_wdn_{i}", [128, 512], BF16) for i in range(NKB)]
    acst8 = [mk.sb(f"mo_acst_{i}", [128, 512], BF16) for i in range(NKB)]
    wst = mk.sb("mo_wst", [128, 2048], F32)
    rwb = mk.sb("mo_rwb", [128, 64], BF16)
    rwf = mk.sb("mo_rwf", [128, 64], F32)
    xf32 = [mk.sb(f"mo_xf32_{i}", [128, 128], F32) for i in range(NKB)]
    gst = mk.sb("mo_gst", [128, 512], F32)
    slst = mk.sb("mo_slst", [128, 512], F32)
    ust = mk.sb("mo_ust", [128, 512], F32)
    u2st = mk.sb("mo_u2st", [128, 512], F32)
    m1st = mk.sb("mo_m1st", [128, 512], F32)
    act16 = mk.sb("mo_act16", [128, 512], BF16)
    bgc = mk.sb("mo_bgc", [128, 64], F32)
    buc = mk.sb("mo_buc", [128, 64], F32)
    wT = mk.sb("mo_wT", [8, T], F32)
    lgall = mk.sb("mo_lgall", [128, 128], F32)
    e1a = mk.sb("mo_e1a", [128, 128], F32)
    e2a = mk.sb("mo_e2a", [128, 128], F32)
    mka = mk.sb("mo_mka", [128, 128], F32)
    wca = mk.sb("mo_wca", [128, 128], F32)
    rstat = mk.sb("mo_rstat", [128, 80], F32)
    wT16 = mk.sb("mo_wT16", [8, T], BF16)
    rst = mk.sb("mo_rst", [128, 10], F32)
    rmsk = mk.sb("mo_rmsk", [128, 8], F32)
    rlog = mk.sb("mo_rlog", [128, 8], F32)
    re1 = mk.sb("mo_re1", [128, 8], F32)
    onesf_s = mk.sb("mo_onesf_s", [1, 128], F32)
    idf = mk.sb("mo_idf", [128, 128], F32)
    dBf = mk.sb("mo_dBf", [8, 1024], F32)
    dB = mk.sb("mo_dB", [8, 1024], BF16)
    rbuf = [mk.sb(f"mo_r_{i}", [128, 512], F32) for i in range(NKB)]
    xres = [mk.sb(f"mo_xh_{i}", [128, 512], F32) for i in range(NKB)]
    ln = alloc_ln_scratch(mk, 256, "moln")

    pG = mk.ps("mo_pG", [128, 512])[:, :]
    pU = mk.ps("mo_pU", [128, 512])[:, :]
    pdown = [mk.ps(f"mo_pd_{i}", [128, 512])[:, :] for i in range(4)]

    wrow = mk.sb("mo_wrow", [1, T], F32)
    if dbg is not None:
        act_dr = mk.nc.dram_tensor("act_dr", [NE, D, T], BF16, kind="ExternalOutput").ap()
    else:
        act_dr = mk.dtmp("mo_act_dr", [NE, D, T], BF16)
    down_dr = mk.dtmp("mo_down_dr", [NE, D, D], BF16)
    wT_dr = (mk.nc.dram_tensor("wT_dr", [8, T], F32, kind="ExternalOutput").ap()
             if dbg is not None else mk.dtmp("mo_wT_dr", [8, T], F32))
    r_dr = mk.nc.dram_tensor("r_dr", [D, T], F32,
                             kind="ExternalOutput" if dbg is not None else "Internal").ap()

    init_ln_scratch(mk, ln)
    ve.memset(onesf_s[:], 1.0 / AL)
    ins = sy.dma_start(idf[:], wd["idf"]); mk.inc(ins, "Wd", 16)
    ins = sy.dma_start(bgc[:], wd["bgc"]); mk.inc(ins, "Wd", 16)
    ins = sy.dma_start(buc[:], wd["buc"]); mk.inc(ins, "Wd", 16)
    ins = sy.dma_start(dBf[:], wd["dbias"]); wv_ = mk.inc(ins, "Wd", 16)
    mk.wait(ve, "Wd", wv_)
    ins = ve.tensor_copy(dB[:], dBf[:])
    vc_db = mk.inc(ins, "Vc")

    for i in range(NKB):
        ins = sy.dma_start(wst[:, 0:T], x3_dram[i * 128:(i + 1) * 128, :])
        xd = mk.inc(ins, "Xd", 16)
        mk.wait(ve, "Xd", xd)
        ins = ve.tensor_copy(x16[i][:], wst[:, 0:T])
        vc_x = mk.inc(ins, "Vc")
        mk.wait(sy, "Vc", vc_x)
    vc_x16 = mk.ctr["Vc"]

    for kb in range(NKB):
        ins = sy.dma_start(rwf[:, kb * 8:(kb + 1) * 8], wd["router"][kb * 128:(kb + 1) * 128, :])
        mk.inc(ins, "Wd", 16)
    vc_rw = mk.ctr["Vc"]

    mk.wait(te, "Vc", vc_rw)
    mk.wait(te, "Wd", mk.ctr["Wd"])
    mk.wait(ve, "Wd", mk.ctr["Wd"])
    mk.wait(sy, "Wd", mk.ctr["Wd"])
    tm_prev = 0
    for tb in range(16):
        pj = pG[:, 0:8]
        mk.wait(te, "Vc", mk.ctr["Vc"])
        if tm_prev:
            mk.wait(sy, "Tm", tm_prev)
        for kb in range(NKB):
            ins = sy.dma_start(xf32[kb][:], x3_dram[kb * 128:(kb + 1) * 128,
                                                    tb * 128:(tb + 1) * 128])
            xd_ = mk.inc(ins, "Xd", 16)
        mk.wait(te, "Xd", xd_)
        for kb in range(NKB):
            ins = te.matmul(pj, xf32[kb][:], rwf[:, kb * 8:(kb + 1) * 8],
                            start=(kb == 0), stop=(kb == NKB - 1))
        tm_prev = tml = mk.inc(ins, "Tm")
        mk.wait(ve, "Tm", tml)
        ins = ve.tensor_copy(lgall[:, tb * 8:(tb + 1) * 8], pj)
        mk.inc(ins, "Vc")
    lg3 = lgall[:].rearrange("p (t e) -> p t e", e=8)
    e13 = e1a[:].rearrange("p (t e) -> p t e", e=8)
    e23 = e2a[:].rearrange("p (t e) -> p t e", e=8)
    mk3 = mka[:].rearrange("p (t e) -> p t e", e=8)

    def bc16(ap2d):
        return ap2d.rearrange("p (t o) -> p t o", o=1).broadcast_to([128, 16, 8])

    ins = ve.reduce_max(rstat[:, 0:16], lg3, axis=AX)
    vsync(mk, ve, ins)
    ins = ve.tensor_tensor(mk3, lg3, bc16(rstat[:, 0:16]), ALU.is_ge)
    vsync(mk, ve, ins)
    ins = ve.scalar_tensor_tensor(e1a[:], mka[:], -1e30, lgall[:], ALU.mult, ALU.add)
    vsync(mk, ve, ins)
    ins = ve.reduce_max(rstat[:, 16:32], e13, axis=AX)
    vsync(mk, ve, ins)
    ins = ve.tensor_tensor(e13, lg3, bc16(rstat[:, 0:16]), ALU.subtract)
    ins = ve.tensor_sub(rstat[:, 32:48], rstat[:, 16:32], rstat[:, 0:16])
    vc_pe = mk.inc(ins, "Vc")
    mk.wait(sc, "Vc", vc_pe)
    sc.activation(e2a[:], e1a[:], AF.Exp)
    ins = sc.activation(rstat[:, 48:64], rstat[:, 32:48], AF.Exp)
    sg_ex = mk.inc(ins, "Sg")
    mk.wait(ve, "Sg", sg_ex)
    ins = ve.tensor_scalar_add(rstat[:, 48:64], rstat[:, 48:64], 1.0)
    vsync(mk, ve, ins)
    ins = ve.reciprocal(rstat[:, 64:80], rstat[:, 48:64])
    vsync(mk, ve, ins)
    ins = ve.tensor_tensor(mk3, lg3, bc16(rstat[:, 16:32]), ALU.is_ge)
    vsync(mk, ve, ins)
    ins = ve.tensor_tensor(e23, e23, bc16(rstat[:, 64:80]), ALU.mult)
    vsync(mk, ve, ins)
    ins = ve.tensor_mul(wca[:], e2a[:], mka[:])
    vc_w = mk.inc(ins, "Vc")
    mk.wait(ve, "Vc", vc_w)
    mk.wait(te, "Vc", vc_w)
    for tb in range(16):
        mk.wait(te, "Vc", mk.ctr["Vc"])
        ins = te.transpose(pU[0:8, 0:128], wca[:, tb * 8:(tb + 1) * 8], idf[:])
        tmt = mk.inc(ins, "Tm")
        mk.wait(ve, "Tm", tmt)
        ve.tensor_copy(wT[0:8, tb * 128:(tb + 1) * 128], pU[0:8, 0:128])
        ins = ve.tensor_copy(wT16[0:8, tb * 128:(tb + 1) * 128], pU[0:8, 0:128])
        mk.inc(ins, "Vc")
    vc_router = mk.ctr["Vc"]
    mk.wait(sy, "Vc", vc_router)
    ins = sy.dma_start(wT_dr, wT[0:8, :])
    wt_d = mk.inc(ins, "Ad", 16)
    if dbg is not None:
        ins = sy.dma_start(dbg["wT"], wT[0:8, :])
        mk.inc(ins, "Od", 16)

    import os
    JOIN = os.environ.get('JOIN_GLU', '0') == '1'
    nc_engs = (te, ve, sc, gp, sy)
    if JOIN:
        mk.join(*nc_engs)
    for e in range(NE):
        for wbuf, wsrc in [(wg16, wd["gate"]), (wu16, wd["up"])]:
            for i in range(NKB):
                mk.wait(sy, "Wc", mk.ctr["Wc"])
                mk.wait(sy, "Tm", mk.ctr["Tm"])
                ins = sy.dma_start(wst[:, 0:1024], wsrc[e, i * 128:(i + 1) * 128, :])
                wv_ = mk.inc(ins, "Wd", 16)
                eng = sc if i % 2 == 0 else ve
                mk.wait(eng, "Tm", mk.ctr["Tm"])
                mk.wait(eng, "Wd", wv_)
                if eng is sc:
                    ins = eng.copy(wbuf[i][:], wst[:, 0:1024])
                else:
                    ins = eng.tensor_copy(wbuf[i][:], wst[:, 0:1024])
                mk.inc(ins, "Wc")
        wc_e = mk.ctr["Wc"]
        if JOIN:
            mk.join(*nc_engs)
        mk.wait(sy, "Ad", wt_d)
        mk.wait(sy, "Tm", mk.ctr["Tm"])
        ins = sy.dma_start(wrow[:], wT_dr[e:e + 1, :])
        wr_d = mk.inc(ins, "Ad", 16)
        mk.wait(te, "Ad", wr_d)
        mk.wait(te, "Wc", wc_e)
        for tc in range(4):
            if JOIN:
                mk.join(*nc_engs)
            ts4 = slice(tc * 512, (tc + 1) * 512)
            mk.wait(te, "Vc", mk.ctr["Vc"])
            ins = te.matmul(pdown[0], onesf_s[:], wrow[0:1, ts4], start=True, stop=True)
            tm_wb = mk.inc(ins, "Tm")
            for fb in range(NKB):
                if os.environ.get('JOIN_FB', '0') == '1':
                    mk.join(*nc_engs)
                mk.wait(te, "Vc", mk.ctr["Vc"])
                mk.wait(te, "Gc", mk.ctr["Gc"])
                for kb in range(NKB):
                    te.matmul(pG, wg16[kb][:, fb * 128:(fb + 1) * 128],
                              x16[kb][:, ts4], start=(kb == 0), stop=(kb == NKB - 1))
                for kb in range(NKB):
                    ins = te.matmul(pU, wu16[kb][:, fb * 128:(fb + 1) * 128],
                                    x16[kb][:, ts4], start=(kb == 0), stop=(kb == NKB - 1))
                tm_gu = mk.inc(ins, "Tm")
                mk.wait(ve, "Tm", tm_gu)
                col = e * 8 + fb
                mk.wait(ve, "Sg", mk.ctr["Sg"])
                mk.wait(ve, "Gc", mk.ctr["Gc"])
                ve.tensor_scalar(gst[:], pG, bgc[:, col:col + 1], 7.0, ALU.add, ALU.min)
                ins = ve.tensor_scalar(ust[:], pU, buc[:, col:col + 1], -7.0, ALU.add, ALU.max)
                vc_g = mk.inc(ins, "Vc")
                mk.wait(sc, "Vc", vc_g)
                ins = sc.activation(slst[:], gst[:], AF.Silu, scale=AL)
                sg_sl = mk.inc(ins, "Sg")
                ve.tensor_scalar(u2st[:], ust[:], 7.0, 1.0, ALU.min, ALU.add)
                mk.wait(ve, "Sg", sg_sl)
                ins = ve.tensor_mul(m1st[:], slst[:], u2st[:])
                mk.wait(ve, "Tm", tm_wb)
                mk.wait(ve, "Ad", mk.ctr["Ad"])
                ins = ve.tensor_mul(act16[:], m1st[:], pdown[0])
                vc_a = mk.inc(ins, "Vc")
                mk.wait(sy, "Vc", vc_a)
                ins = sy.dma_start(act_dr[e, fb * 128:(fb + 1) * 128, ts4], act16[:])
                mk.inc(ins, "Ad", 16)
                if dbg is not None and e == 0 and fb == 0 and tc == 0:
                    ins = sy.dma_start(dbg["act"], act16[:])
                    mk.inc(ins, "Od", 16)

    if os.environ.get('JOIN_DOWN', '0') == '1':
        mk.join(te, ve, sc, gp, sy)
    for e in range(NE):
        for i in range(NKB):
            mk.wait(sy, "Wc", mk.ctr["Wc"])
            ins = sy.dma_start(wst[:, 0:1024], wd["down"][e, i * 128:(i + 1) * 128, :])
            wv_ = mk.inc(ins, "Wd", 16)
            eng = sc if i % 2 == 0 else ve
            mk.wait(eng, "Tm", mk.ctr["Tm"])
            mk.wait(eng, "Wd", wv_)
            mk.wait(eng, "Ad", mk.ctr["Ad"])
            if eng is sc:
                ins = eng.copy(wg16[0][:], wst[:, 0:1024])
            else:
                ins = eng.tensor_copy(wg16[0][:], wst[:, 0:1024])
            cc = mk.inc(ins, "Wc")
            mk.wait(sy, "Wc", cc)
            ins = sy.dma_start(down_dr[e, i * 128:(i + 1) * 128, :], wg16[0][:])
            mk.inc(ins, "Ad", 16)
    ad_all = mk.ctr["Ad"]

    mk.wait(sy, "Ad", ad_all)
    for tc in range(4):
        ts4 = slice(tc * 512, (tc + 1) * 512)
        mk.wait(sy, "Vc", mk.ctr["Vc"])
        for i in range(NKB):
            ins = sy.dma_start(xres[i][:], x3_dram[i * 128:(i + 1) * 128, ts4])
            xd_r = mk.inc(ins, "Xd", 16)
        for hhalf in range(2):
            mk.wait(te, "Vc", max(vc_db, mk.ctr["Vc"]) if False else vc_db)
            mk.wait(te, "Vc", mk.ctr["Vc"])
            mk.wait(te, "Sg", mk.ctr["Sg"])
            for m4 in range(4):
                mb = hhalf * 4 + m4
                ins = te.matmul(pdown[m4], dB[0:8, mb * 128:(mb + 1) * 128],
                                wT16[0:8, ts4], start=True, stop=True)
            tm_dn = mk.inc(ins, "Tm")
            mk.wait(sc, "Tm", tm_dn)
            mk.wait(sc, "Vc", mk.ctr["Vc"])
            mk.wait(sc, "Od", mk.ctr["Od"])
            for m4 in range(4):
                mb = hhalf * 4 + m4
                ins = sc.copy(rbuf[mb][:], pdown[m4])
                mk.inc(ins, "Sg")
            sg_bias = mk.ctr["Sg"]
            for e in range(NE):
                mk.wait(sy, "Tm", mk.ctr["Tm"])
                for fb in range(NKB):
                    ins = sy.dma_start(wdn8[fb][:], down_dr[e, fb * 128:(fb + 1) * 128,
                                                            hhalf * 512:(hhalf + 1) * 512])
                    wv_ = mk.inc(ins, "Wd", 16)
                    ins = sy.dma_start(acst8[fb][:], act_dr[e, fb * 128:(fb + 1) * 128, ts4])
                    ad_ = mk.inc(ins, "Ad", 16)
                mk.wait(te, "Wd", wv_)
                mk.wait(te, "Ad", ad_)
                mk.wait(te, "Sg", sg_bias)
                mk.wait(te, "Vc", mk.ctr["Vc"])
                for fb in range(NKB):
                    for m4 in range(4):
                        ins = te.matmul(pdown[m4], wdn8[fb][:, m4 * 128:(m4 + 1) * 128],
                                        acst8[fb][:], start=(fb == 0), stop=(fb == NKB - 1))
                tm_e = mk.inc(ins, "Tm")
                mk.wait(ve, "Tm", tm_e)
                mk.wait(ve, "Sg", sg_bias)
                for m4 in range(4):
                    mb = hhalf * 4 + m4
                    ins = ve.tensor_add(rbuf[mb][:], rbuf[mb][:], pdown[m4])
                mk.inc(ins, "Vc")
        sg_evac = mk.ctr["Sg"]
        if dbg is not None:
            mk.wait(sy, "Sg", sg_evac)
            for mb in range(NKB):
                ins = sy.dma_start(r_dr[mb * 128:(mb + 1) * 128, ts4], rbuf[mb][:])
                mk.inc(ins, "Rd", 16)
            mk.wait(ve, "Rd", mk.ctr["Rd"])
        mk.wait(ve, "Sg", sg_evac)
        mk.wait(ve, "Xd", xd_r)
        for mb in range(NKB):
            ins = ve.tensor_add(rbuf[mb][:], rbuf[mb][:], xres[mb][:])
        mk.inc(ins, "Vc")
        for half2 in range(2):
            cs = slice(half2 * 256, half2 * 256 + 256)
            vc_apply = build_ln(mk, ln, [r[:, cs] for r in rbuf], [r[:, cs] for r in rbuf])
            mk.wait(sy, "Vc", vc_apply)
            for i in range(NKB):
                ins = sy.dma_start(out_dram[i * 128:(i + 1) * 128,
                                            tc * 512 + half2 * 256:tc * 512 + half2 * 256 + 256],
                                   rbuf[i][:, cs])
                mk.inc(ins, "Od", 16)
    if dbg is not None:
        mk.join(te, ve, sc, gp, sy)
        for i in range(NKB):
            ins = sy.dma_start(dbg["x16d"][:, i * T:(i + 1) * T], x16[i][:])
            mk.inc(ins, "Od", 16)
    mk.wait(sy, "Od", mk.ctr["Od"])


def host_moe_weights(router_weight, gate_up_proj, gate_up_proj_bias, down_proj,
                     down_proj_bias):
    gate = np.ascontiguousarray(gate_up_proj[:, :, 0::2])
    up = np.ascontiguousarray(gate_up_proj[:, :, 1::2])
    bg = gate_up_proj_bias[:, 0::2]
    bu = gate_up_proj_bias[:, 1::2]
    bgc = np.zeros((128, 64), np.float32)
    buc = np.zeros((128, 64), np.float32)
    for e in range(8):
        for fb in range(8):
            bgc[:, e * 8 + fb] = bg[e, fb * 128:(fb + 1) * 128]
            buc[:, e * 8 + fb] = bu[e, fb * 128:(fb + 1) * 128]
    return dict(router=np.ascontiguousarray(router_weight),
                gate=gate, up=up, bgc=bgc, buc=buc,
                down=np.ascontiguousarray(down_proj),
                dbias=np.ascontiguousarray(down_proj_bias),
                idf=np.eye(128, dtype=np.float32))



def build_l0_phase(mk, x0_dram, x1_dram, wd, dbg=None):
    import os
    STAGE = int(os.environ.get("L0_STAGE", "9"))
    nc = mk.nc
    sy, te, ve, sc, gp = nc.sync, nc.tensor, nc.vector, nc.scalar, nc.gpsimd
    SC = 256
    NSC = int(os.environ.get("L0_NSC", str(T // SC)))
    NH = 16

    x0T = [mk.sb(f"l0_x0_{i}", [128, T], F32) for i in range(NKB)]
    maac = {g: mk.sb(f"l0_maa_{g}", [128, 8], F32) for g in "xwkvrg"}
    tdc = mk.sb("l0_tdc", [128, 8], F32)
    uc = mk.sb("l0_uc", [128, 8], F32)
    gamr = mk.sb("l0_gamr", [1, 1024], F32)
    betr = mk.sb("l0_betr", [1, 1024], F32)
    gbc = mk.sb("l0_gbc", [128, 1024], F32)
    bbc = mk.sb("l0_bbc", [128, 1024], F32)
    maaw1 = [mk.sb(f"l0_mw1_{i}", [128, 160], BF16) for i in range(NKB)]
    maaw2 = mk.sb("l0_mw2", [32, 5 * 1024], BF16)
    dw1 = mk.sb("l0_dw1", [128, 8 * 64], BF16)
    dw2 = mk.sb("l0_dw2", [64, 1024], BF16)
    hsel = mk.sb("l0_hsel", [128, 128], BF16)
    i64t = mk.sb("l0_i64t", [64, 1024], BF16)
    ust = mk.sb("l0_ust", [64, 512], BF16)
    rmask = mk.sb("l0_rmask", [128, 256], F32)
    idf = mk.sb("l0_idf", [128, 128], F32)
    id16 = mk.sb("l0_id16", [128, 128], BF16)
    wstf = mk.sb("l0_wstf", [128, 512], F32)
    wcb = [mk.sb(f"l0_wcb_{i}", [128, 1024], BF16) for i in range(2)]
    wc16 = wcb[0]
    vgw = [mk.sb(f"l0_vgw_{i}", [128, 1024], BF16) for i in range(NKB)]
    xxT = [vgw[i][:].bitcast(F32)[:, 0:SC] for i in range(NKB)]
    xsb = [vgw[i][:].bitcast(F32)[:, SC:2 * SC] for i in range(NKB)]
    pcol = mk.sb("l0_pcol", [128, 8], F32)
    Xg16 = [mk.sb(f"l0_xg_{i}", [128, SC], BF16) for i in range(NKB)]
    Xv16 = [mk.sb(f"l0_xv_{i}", [128, SC], BF16) for i in range(NKB)]
    Xgg16 = [mk.sb(f"l0_xgg_{i}", [128, SC], BF16) for i in range(NKB)]
    mt16 = [mk.sb(f"l0_mt_{i}", [32, SC], BF16) for i in range(5)]
    td16 = mk.sb("l0_td16", [64, SC], BF16)
    r16 = [mk.sb(f"l0_r16_{i}", [128, SC], BF16) for i in range(NKB)]
    k16 = [mk.sb(f"l0_k16_{i}", [128, SC], BF16) for i in range(NKB)]
    Kt16 = [mk.sb(f"l0_kt16_{i}", [128, SC], BF16) for i in range(NKB)]
    eWT = [mk.sb(f"l0_ew_{i}", [128, SC], F32) for i in range(NKB)]
    cT = [mk.sb(f"l0_c_{i}", [128, SC], F32) for i in range(NKB)]
    ctot4 = [mk.sb(f"l0_ct4_{i}", [128, 4], F32) for i in range(NKB)]
    ect4 = [mk.sb(f"l0_ec4_{i}", [128, 4], F32) for i in range(NKB)]
    wtE4 = mk.sb("l0_wte4", [128, 32], F32)
    vA = [mk.sb(f"l0_va_{i}", [64, 1024], BF16) for i in range(4)]
    gA = [mk.sb(f"l0_ga_{i}", [128, 1024], BF16) for i in range(2)]
    rku16 = [mk.sb(f"l0_rku_{i}", [128, SC], BF16) for i in range(NKB)]
    dsb = mk.sb("l0_dsb", [16, SC], F32)
    dcm = mk.sb("l0_dcm", [64, 64], F32)
    PT16 = mk.sb("l0_pt16", [64, 1024], BF16)
    Dg16 = mk.sb("l0_dg16", [64, 1024], BF16)
    KtA = mk.sb("l0_kta", [64, 1024], BF16)
    S = mk.sb("l0_S", [128, 512], F32)
    S16 = mk.sb("l0_S16", [128, 512], BF16)
    y2 = mk.sb("l0_y2", [128, 1024], F32)
    ybsb = mk.sb("l0_ybsb", [128, 1024], F32)
    ysq = ybsb
    gns = mk.sb("l0_gns", [128, 96], F32)


    ygT16 = Xv16
    rbuf = eWT
    PJ = [mk.ps(f"l0_pj_{i}", [128, 512])[:, :] for i in range(2)]
    PA = mk.ps("l0_pa", [128, 1024])[:, :]
    PB = mk.ps("l0_pb", [128, 1024])[:, :]
    PSb = [mk.ps(f"l0_ps_{i}", [64, 512])[:, :] for i in range(2)]
    ln = alloc_ln_scratch(mk, SC, "l0ln", pst=PSb[0][0:1, 0:2 * SC], pbcb=PJ[0],
                          r16s=[cT[i][:].bitcast(BF16)[:, 0:SC] for i in range(NKB)])
    wscr = mk.dtmp("l0_wscr", [5, D, D], BF16)

    init_ln_scratch(mk, ln)
    for g in "xwkvrg":
        ins = sy.dma_start(maac[g][:], wd["maa_" + g]); mk.inc(ins, "Wd", 16)
    for nm, dst in [("tdc", tdc), ("uc", uc), ("gamr", gamr), ("betr", betr),
                    ("hsel", hsel), ("i64t", i64t), ("ust", ust), ("rmask", rmask),
                    ("idf", idf), ("id16", id16)]:
        ins = sy.dma_start(dst[:], wd[nm]); mk.inc(ins, "Wd", 16)
    wd_const = mk.ctr["Wd"]
    small = []
    for i in range(NKB):
        small.append((wd["maaw1"][i * 128:(i + 1) * 128, :], maaw1[i][:], 128, 160))
    for g5 in range(5):
        small.append((wd["maaw2"][g5], maaw2[:, g5 * 1024:(g5 + 1) * 1024], 32, 1024))
    for i in range(NKB):
        small.append((wd["dw1"][i * 128:(i + 1) * 128, :], dw1[:, i * 64:(i + 1) * 64], 128, 64))
    small.append((wd["dw2"], dw2[:], 64, 1024))
    for (dsl, dst, npart, ncols) in small:
        for q0 in range(0, ncols, 512):
            qn = min(512, ncols - q0)
            mk.wait(sy, "Wc", mk.ctr["Wc"])
            ins = sy.dma_start(wstf[0:npart, 0:qn], dsl[:, q0:q0 + qn])
            wv_ = mk.inc(ins, "Wd", 16)
            mk.wait(ve, "Wd", wv_)
            ins = ve.tensor_copy(dst[:, q0:q0 + qn], wstf[0:npart, 0:qn])
            mk.inc(ins, "Wc")
    for wi, wn in enumerate(["W_r", "W_k", "W_v", "W_g", "W_o"]):
        for i in range(NKB):
            for q0 in range(0, 1024, 512):
                mk.wait(sy, "Wc", mk.ctr["Wc"])
                ins = sy.dma_start(wstf[:], wd[wn][i * 128:(i + 1) * 128, q0:q0 + 512])
                wv_ = mk.inc(ins, "Wd", 16)
                eng = sc if i % 2 == 0 else ve
                mk.wait(eng, "Wd", wv_)
                mk.wait(eng, "Ad", mk.ctr["Ad"])
                dstw = wc16[:, q0:q0 + 512]
                if wn == "W_o":
                    ins = eng.mul(dstw, wstf[:], 0.5) if eng is sc else \
                          eng.tensor_scalar_mul(dstw, wstf[:], 0.5)
                else:
                    ins = eng.copy(dstw, wstf[:]) if eng is sc else \
                          eng.tensor_copy(dstw, wstf[:])
                mk.inc(ins, "Wc")
            cc_ = mk.ctr["Wc"]
            mk.wait(gp, "Wc", cc_)
            ins = gp.dma_start(wscr[wi, i * 128:(i + 1) * 128, :], wc16[:])
            mk.inc(ins, "Ad", 16)
    ad_wscr = mk.ctr["Ad"]
    mk.wait(te, "Wd", wd_const)
    mk.wait(te, "Wc", mk.ctr["Wc"])
    for src_r, dst_b in [(gamr, gbc), (betr, bbc)]:
        for half in range(2):
            mk.wait(te, "Vc", mk.ctr["Vc"])
            ins = te.matmul(PJ[1][:, 0:512], ln["onesf"][:],
                            src_r[0:1, half * 512:(half + 1) * 512], start=True, stop=True)
            tmg = mk.inc(ins, "Tm")
            mk.wait(ve, "Tm", tmg)
            ins = ve.tensor_copy(dst_b[:, half * 512:(half + 1) * 512], PJ[1][:, 0:512])
            mk.inc(ins, "Vc")
    ins = ve.memset(pcol[:], 0.0)
    vsync(mk, ve, ins)
    ins = ve.memset(S[:], 0.0)
    vsync(mk, ve, ins)
    ins = ve.memset(S16[:], 0.0)
    vsync(mk, ve, ins)

    for i in range(NKB):
        ins = gp.dma_start(x0T[i][:], x0_dram[i * 128:(i + 1) * 128, :])
        mk.inc(ins, "Xd", 16)
    xd_x0 = mk.ctr["Xd"]
    mk.wait(ve, "Xd", xd_x0)
    mk.wait(te, "Xd", xd_x0)
    mk.wait(sy, "Ad", ad_wscr)

    def stream_wcol(wi, mb, j):
        sl = wscr[wi][:, mb * 128:(mb + 1) * 128].rearrange("(kb p) c -> p kb c", p=128)
        dst = wcb[j][:].rearrange("p (kb c) -> p kb c", c=128)
        ins = sy.dma_start(dst, sl)
        return mk.inc(ins, "Wd", 16)

    for sci in range(NSC):
        t0 = sci * SC
        ts4 = slice(t0, t0 + SC)
        mk.wait(ve, "Tm", mk.ctr["Tm"])
        mk.wait(ve, "Wd", mk.ctr["Wd"])
        for i in range(NKB):
            ve.tensor_sub(xxT[i][:, 1:SC], x0T[i][:, t0:t0 + SC - 1], x0T[i][:, t0 + 1:t0 + SC])
            ins = ve.tensor_sub(xxT[i][:, 0:1], pcol[:, i:i + 1], x0T[i][:, t0:t0 + 1])
            vsync(mk, ve, ins)
            ins = ve.tensor_copy(pcol[:, i:i + 1], x0T[i][:, t0 + SC - 1:t0 + SC])
            vsync(mk, ve, ins)
            ins = ve.scalar_tensor_tensor(Xg16[i][:], xxT[i][:], maac["x"][:, i:i + 1],
                                          x0T[i][:, ts4], ALU.mult, ALU.add)
        vc_xxx = mk.inc(ins, "Vc")
        if dbg is not None and sci == 0:
            mk.wait(gp, "Vc", vc_xxx)
            for i_ in range(NKB):
                ins = gp.dma_start(dbg["xx"][i_ * 128:(i_ + 1) * 128, :], xxT[i_][:])
                mk.inc(ins, "Od", 16)
            ins = gp.dma_start(dbg["xxx"], Xg16[0][:])
            od_xx = mk.inc(ins, "Od", 16)
            mk.wait(ve, "Od", od_xx)
            mk.wait(sy, "Od", od_xx)
        mk.wait(te, "Vc", vc_xxx)
        mk.wait(te, "Sg", mk.ctr["Sg"])
        for (c0, cn, pj) in [(0, 64, PSb[0][0:64, 0:SC]), (64, 64, PSb[1][0:64, 0:SC]),
                             (128, 32, PJ[1][0:32, 0:SC])]:
            for kb in range(NKB):
                ins = te.matmul(pj, maaw1[kb][:, c0:c0 + cn], Xg16[kb][:],
                                start=(kb == 0), stop=(kb == NKB - 1))
            tm_m = mk.inc(ins, "Tm")
            mk.wait(sc, "Tm", tm_m)
            for h2 in range(cn // 32):
                g5 = (c0 + h2 * 32) // 32
                ins = sc.activation(mt16[g5][:], pj[h2 * 32:h2 * 32 + 32, :], AF.Tanh)
            mk.inc(ins, "Sg")
        sg_mt = mk.ctr["Sg"]
        if dbg is not None and sci == 0:
            mk.wait(gp, "Sg", sg_mt)
            for g5 in range(5):
                ins = gp.dma_start(dbg["mt"][:, g5 * 256:(g5 + 1) * 256], mt16[g5][:])
                mk.inc(ins, "Od", 16)
            od_mt = mk.inc(gp.engine_nop(), "Gc") if False else mk.ctr["Od"]
            mk.wait(sc, "Od", od_mt)
            mk.wait(te, "Od", od_mt)

        def assemble_gate(gi, g, dst16):
            mk.wait(te, "Sg", sg_mt)
            for mb in range(NKB):
                pj = PJ[mb % 2][:, 0:SC]
                mk.wait(te, "Vc", mk.ctr["Vc"])
                ins = te.matmul(pj, maaw2[:, gi * 1024 + mb * 128:gi * 1024 + (mb + 1) * 128],
                                mt16[gi][:], start=True, stop=True)
                tm_mg = mk.inc(ins, "Tm")
                mk.wait(ve, "Tm", tm_mg)
                ve.scalar_tensor_tensor(xsb[mb][:], pj, maac[g][:, mb:mb + 1], xxT[mb][:],
                                        ALU.add, ALU.mult)
                ins = ve.tensor_add(dst16[mb][:], xsb[mb][:], x0T[mb][:, ts4])
                mk.inc(ins, "Vc")
            return mk.ctr["Vc"]

        vc_g = assemble_gate(0, "w", Xg16)
        mk.wait(te, "Vc", vc_g)
        for kb in range(NKB):
            ins = te.matmul(PSb[0][0:64, 0:SC], dw1[:, kb * 64:(kb + 1) * 64], Xg16[kb][:],
                            start=(kb == 0), stop=(kb == NKB - 1))
        tm_t1 = mk.inc(ins, "Tm")
        mk.wait(sc, "Tm", tm_t1)
        ins = sc.activation(td16[:], PSb[0][0:64, 0:SC], AF.Tanh)
        sg_t1 = mk.inc(ins, "Sg")
        mk.wait(te, "Sg", sg_t1)
        for mb in range(NKB):
            pj = PJ[mb % 2][:, 0:SC]
            mk.wait(te, "Sg", mk.ctr["Sg"])
            mk.wait(te, "Vc", mk.ctr["Vc"])
            ins = te.matmul(pj, dw2[:, mb * 128:(mb + 1) * 128], td16[:], start=True, stop=True)
            tm_d = mk.inc(ins, "Tm")
            mk.wait(sc, "Tm", tm_d)
            mk.wait(sc, "Vc", mk.ctr["Vc"])
            mk.wait(sc, "Od", mk.ctr["Od"])
            ins = sc.activation(eWT[mb][:], pj, AF.Exp, bias=tdc[:, mb:mb + 1])
            mk.inc(ins, "Sg")
        sg_eW = mk.ctr["Sg"]

        for g, dstP in [("k", k16), ("r", r16)]:
            gi = {"k": 1, "r": 3}[g]
            wi = {"k": 1, "r": 0}[g]
            vc_g = assemble_gate(gi, g, Xg16)
            mk.wait(te, "Vc", vc_g)
            for mb in range(NKB):
                j = mb % 2
                mk.wait(sy, "Tm", mk.ctr["Tm"] if mb < 2 else tm_wp[j])
                wv_ = stream_wcol(wi, mb, j)
                pj = PJ[mb % 2][:, 0:SC]
                mk.wait(te, "Wd", wv_)
                mk.wait(te, "Sg", mk.ctr["Sg"])
                for kb in range(NKB):
                    ins = te.matmul(pj, wcb[j][:, kb * 128:(kb + 1) * 128], Xg16[kb][:],
                                    start=(kb == 0), stop=(kb == NKB - 1))
                if mb < 2:
                    tm_wp = [0, 0]
                tm_wp[j] = mk.inc(ins, "Tm")
                mk.wait(sc, "Tm", tm_wp[j])
                ins = sc.copy(dstP[mb][:], pj)
                sg_pr = mk.inc(ins, "Sg")
                if dbg is not None and sci == 0:
                    mk.wait(gp, "Sg", sg_pr)
                    ins = gp.dma_start(dbg[g + "16"][mb * 128:(mb + 1) * 128, :], dstP[mb][:])
                    od_pr = mk.inc(ins, "Od", 16)
                    mk.wait(sc, "Od", od_pr)

        if STAGE < 2:
            continue
        assemble_gate(2, "v", Xv16)
        vc_g = assemble_gate(4, "g", Xgg16)
        for wi, X16, is_v in [(2, Xv16, True), (3, Xgg16, False)]:
            mk.wait(sy, "Vc", mk.ctr["Vc"])
            mk.wait(sy, "Tm", mk.ctr["Tm"])
            for kb in range(NKB):
                ins = sy.dma_start(vgw[kb][:], wscr[wi, kb * 128:(kb + 1) * 128, :])
                wv_ = mk.inc(ins, "Wd", 16)
            mk.wait(te, "Wd", wv_)
            mk.wait(te, "Vc", mk.ctr["Vc"])
            if is_v:
                for cc in range(4):
                    for half in range(2):
                        pj = PSb[half][0:64, 0:512]
                        mk.wait(te, "Sg", mk.ctr["Sg"])
                        for kb in range(NKB):
                            ins = te.matmul(pj, X16[kb][:, cc * 64:(cc + 1) * 64],
                                            vgw[kb][:, half * 512:(half + 1) * 512],
                                            start=(kb == 0), stop=(kb == NKB - 1))
                        tm_v = mk.inc(ins, "Tm")
                        mk.wait(sc, "Tm", tm_v)
                        ins = sc.copy(vA[cc][:, half * 512:(half + 1) * 512], pj)
                        mk.inc(ins, "Sg")
            else:
                for tb2 in range(2):
                    for half in range(2):
                        pj = PJ[half][:, 0:512]
                        mk.wait(te, "Sg", mk.ctr["Sg"])
                        mk.wait(te, "Vc", mk.ctr["Vc"])
                        for kb in range(NKB):
                            ins = te.matmul(pj, X16[kb][:, tb2 * 128:(tb2 + 1) * 128],
                                            vgw[kb][:, half * 512:(half + 1) * 512],
                                            start=(kb == 0), stop=(kb == NKB - 1))
                        tm_g = mk.inc(ins, "Tm")
                        mk.wait(sc, "Tm", tm_g)
                        mk.wait(sc, "Vc", mk.ctr["Vc"])
                        ins = sc.activation(wc16[:, 0:512].bitcast(F32)[:, 0:512] if False
                                            else wstf[:, 0:512], pj, AF.Tanh, scale=0.5)
                        sg_th = mk.inc(ins, "Sg")
                        mk.wait(ve, "Sg", sg_th)
                        mk.wait(ve, "Tm", tm_g)
                        ins = ve.scalar_tensor_tensor(gA[tb2][:, half * 512:(half + 1) * 512],
                                                      wstf[:, 0:512], 1.0, pj,
                                                      ALU.add, ALU.mult)
                        mk.inc(ins, "Vc")

        if dbg is not None and sci == 0 and STAGE >= 2:
            mk.wait(gp, "Sg", mk.ctr["Sg"])
            mk.wait(gp, "Vc", mk.ctr["Vc"])
            for c_ in range(4):
                ins = gp.dma_start(dbg["va"][c_ * 64:(c_ + 1) * 64, :], vA[c_][:])
                mk.inc(ins, "Od", 16)
            ins = gp.dma_start(dbg["ga"], gA[0][:])
            od_g = mk.inc(ins, "Od", 16)
            mk.wait(ve, "Od", od_g)
        if STAGE < 3:
            continue
        mk.wait(ve, "Sg", sg_eW)
        for mb in range(NKB):
            ins = ve.tensor_tensor_scan(cT[mb][:], rmask[:], eWT[mb][:], 0.0,
                                        ALU.mult, ALU.add)
        vc_c = mk.inc(ins, "Vc")
        mk.wait(ve, "Vc", vc_c)
        for mb in range(NKB):
            ve.tensor_copy(ctot4[mb][:], cT[mb][:].rearrange("p (c l) -> p c l", l=64)[:, :, 63:64])
            ins = ve.tensor_sub(eWT[mb][:], eWT[mb][:], cT[mb][:])
        vc_t1 = mk.inc(ins, "Vc")
        mk.wait(sc, "Vc", vc_t1)
        for mb in range(NKB):
            sc.activation(eWT[mb][:], eWT[mb][:], AF.Exp)
            sc.activation(cT[mb][:], cT[mb][:], AF.Exp)
            ins = sc.activation(ect4[mb][:], ctot4[mb][:], AF.Exp, scale=-1.0)
        sg_E = mk.inc(ins, "Sg")
        mk.wait(ve, "Sg", sg_E)
        for mb in range(NKB):
            ve.scalar_tensor_tensor(rku16[mb][:], r16[mb][:], uc[:, mb:mb + 1],
                                    k16[mb][:], ALU.mult, ALU.mult)
            ve.tensor_mul(r16[mb][:], r16[mb][:], eWT[mb][:])
            ve.tensor_mul(k16[mb][:], k16[mb][:], cT[mb][:])
            ect_b = ect4[mb][:].rearrange("p (c o) -> p c o", o=1).broadcast_to([128, 4, 64])
            ve.tensor_tensor(Kt16[mb][:].rearrange("p (c l) -> p c l", l=64),
                             k16[mb][:].rearrange("p (c l) -> p c l", l=64), ect_b, ALU.mult)
            ins = ve.tensor_copy(wtE4[:, mb:32:8], ect4[mb][:])
            vsync(mk, ve, ins)
        vc_fac = mk.inc(ins, "Vc") if False else mk.ctr["Vc"]
        if dbg is not None and sci == 0:
            mk.wait(gp, "Vc", vc_fac)
            for i_ in range(NKB):
                ins = gp.dma_start(dbg["qt"][i_ * 128:(i_ + 1) * 128, :], r16[i_][:])
                mk.inc(ins, "Od", 16)
                ins = gp.dma_start(dbg["kt"][i_ * 128:(i_ + 1) * 128, :], k16[i_][:])
                mk.inc(ins, "Od", 16)
            od_f = mk.ctr["Od"]
            mk.wait(ve, "Od", od_f)
            mk.wait(sc, "Od", od_f)
        mk.wait(te, "Vc", vc_fac)
        mk.wait(te, "Sg", mk.ctr["Sg"])
        for kb in range(NKB):
            ins = te.matmul(PSb[1][0:16, 0:SC], hsel[:, kb * 16:(kb + 1) * 16], rku16[kb][:],
                            start=(kb == 0), stop=(kb == NKB - 1))
        tm_dd = mk.inc(ins, "Tm")
        mk.wait(ve, "Tm", tm_dd)
        ins = ve.tensor_copy(dsb[:], PSb[1][0:16, 0:SC])
        vc_dsb = mk.inc(ins, "Vc")
        mk.wait(te, "Vc", vc_dsb)
        for cc in range(4):
            ins = te.transpose(PSb[1][0:64, cc * 16:(cc + 1) * 16], dsb[:, cc * 64:(cc + 1) * 64],
                               idf[0:16, 0:16])
        tm_dt = mk.inc(ins, "Tm")
        mk.wait(ve, "Tm", tm_dt)
        ins = ve.tensor_copy(dcm[:], PSb[1][0:64, 0:64])
        vc_dcm = mk.inc(ins, "Vc")

        if STAGE < 4:
            continue
        for cc in range(4):
            cs = slice(cc * 64, (cc + 1) * 64)
            p2 = cc % 2
            mk.wait(te, "Vc", mk.ctr["Vc"])
            for h in range(NH):
                mt = h // 2
                hr = slice(64 * (h % 2), 64 * (h % 2) + 64)
                ins = te.matmul(PSb[h % 2][0:64, 64 * (h // 2):64 * (h // 2) + 64],
                                k16[mt][hr, cs], r16[mt][hr, cs], start=True, stop=True)
            tm_sc = mk.inc(ins, "Tm")
            if int(os.environ.get("L0_SUB", "9")) < 2:
                mk.wait(ve, "Tm", tm_sc)
                continue
            mk.wait(ve, "Vc", vc_dcm)
            dcs = dcm[:, cc * 16:(cc + 1) * 16]
            for par in range(2):
                dsl = dcs[:, par::2].rearrange("p (j o) -> p j o", o=1).broadcast_to([64, 8, 64])
                ins = ve.tensor_tensor(
                    Dg16[:, par * 512:(par + 1) * 512].rearrange("p (j l) -> p j l", l=64),
                    i64t[:, 0:512].rearrange("p (j l) -> p j l", l=64), dsl, ALU.mult)
            gc_dg = mk.inc(ins, "Gc")
            mk.wait(ve, "Tm", tm_sc)
            for b in range(2):
                ins = ve.tensor_mul(PT16[:, b * 512:(b + 1) * 512], PSb[b][0:64, 0:512], ust[:])
            mk.inc(ins, "Vc")
            mk.wait(ve, "Gc", gc_dg)
            if int(os.environ.get("L0_SUB", "9")) >= 3:
                ins = ve.tensor_add(PT16[:], PT16[:], Dg16[:])
            vc_pt = mk.inc(ins, "Vc")
            if dbg is not None and sci == 0 and cc == 0:
                mk.wait(gp, "Vc", vc_pt)
                ins = gp.dma_start(dbg["pt0"], PT16[:])
                od_p = mk.inc(ins, "Od", 16)
                mk.wait(te, "Od", od_p)
            if STAGE < 5:
                continue
            mk.wait(te, "Vc", vc_pt)
            mk.wait(te, "Sg", mk.ctr["Sg"])
            mk.wait(te, "Vc", mk.ctr["Vc"])
            yrow = slice(64 * p2, 64 * p2 + 64)
            for h in range(NH):
                hc = slice(64 * h, 64 * h + 64)
                pc = slice(512 * (h % 2) + 64 * (h // 2), 512 * (h % 2) + 64 * (h // 2) + 64)
                ins = te.matmul(PA[yrow, hc], PT16[:, pc], vA[cc][:, hc], start=True, stop=True)
            for h in range(NH):
                mt = h // 2
                hr = slice(64 * (h % 2), 64 * (h % 2) + 64)
                pc = slice(512 * (h % 2) + 64 * (h // 2), 512 * (h % 2) + 64 * (h // 2) + 64)
                ins = te.matmul(PB[yrow, pc], r16[mt][hr, cs],
                                S16[hr, 64 * (h // 2):64 * (h // 2) + 64], start=True, stop=True)
            tm_y = mk.inc(ins, "Tm")
            SKIPDS = os.environ.get("L0_SKIPDS", "0") == "1"
            if SKIPDS:
                pass
            mk.wait(te, "Sg", mk.ctr["Sg"])
            if not SKIPDS:
                for mb in range(NKB):
                    ins = te.transpose(PJ[1].bitcast(BF16)[0:64, mb * 128:(mb + 1) * 128],
                                       Kt16[mb][:, cs], id16[:])
                tm_kt = mk.inc(ins, "Tm")
            if not SKIPDS:
                mk.wait(sc, "Tm", tm_kt)
                ins = sc.copy(KtA[:], PJ[1].bitcast(BF16)[0:64, 0:1024])
                sg_kta = mk.inc(ins, "Sg")
            if not SKIPDS:
                mk.wait(te, "Sg", sg_kta)
                mk.wait(te, "Vc", mk.ctr["Vc"])
                for h in range(NH):
                    hr = slice(64 * (h % 2), 64 * (h % 2) + 64)
                    ins = te.matmul(PJ[0][hr, 64 * (h // 2):64 * (h // 2) + 64],
                                    KtA[:, 64 * h:64 * h + 64], vA[cc][:, 64 * h:64 * h + 64],
                                    start=True, stop=True)
                tm_ds = mk.inc(ins, "Tm")
            if not SKIPDS:
                wtb = wtE4[:, cc * 8:(cc + 1) * 8].rearrange("p (g o) -> p g o", o=1) \
                    .broadcast_to([128, 8, 64])
                mk.wait(ve, "Tm", tm_ds)
                ve.tensor_tensor(S[:].rearrange("p (g l) -> p g l", l=64),
                                 S[:].rearrange("p (g l) -> p g l", l=64), wtb, ALU.mult)
                ve.tensor_add(S[:], S[:], PJ[0])
                ins = ve.tensor_copy(S16[:], S[:])
                vc_su = mk.inc(ins, "Vc")
                if dbg is not None and sci == 0 and cc == 0:
                    mk.wait(gp, "Vc", vc_su)
                    ins = gp.dma_start(dbg["s0"], S[:])
                    od_s = mk.inc(ins, "Od", 16)
                    mk.wait(ve, "Od", od_s)
            if STAGE >= 6:
                mk.wait(sc, "Tm", tm_y)
                mk.wait(sc, "Vc", mk.ctr["Vc"])
                for h_ in range(16):
                    pc_ = 512 * (h_ % 2) + 64 * (h_ // 2)
                    ins = sc.copy(ybsb[yrow, 64 * h_:64 * h_ + 64],
                                  PB[yrow, pc_:pc_ + 64])
                sg_yb = mk.inc(ins, "Sg")
                mk.wait(ve, "Sg", sg_yb)
                mk.wait(ve, "Tm", tm_y)
                ve.tensor_tensor(y2[yrow, 0:512], ybsb[yrow, 0:512], PA[yrow, 0:512], ALU.add)
                ins = ve.tensor_tensor(y2[yrow, 512:1024], ybsb[yrow, 512:1024],
                                       PA[yrow, 512:1024], ALU.add)
                vc_y2 = mk.inc(ins, "Vc")
            if p2 == 1 and STAGE >= 6:
                if dbg is not None and sci == 0 and cc == 1:
                    mk.wait(gp, "Vc", vc_y2)
                    ins = gp.dma_start(dbg["y2r"], y2[:])
                    od_yr = mk.inc(ins, "Od", 16)
                    mk.wait(sc, "Od", od_yr)
                    mk.wait(ve, "Od", od_yr)
                mk.wait(sc, "Vc", vc_y2)
                ins = sc.activation(ysq[:], y2[:], AF.Square)
                sg_sq = mk.inc(ins, "Sg")
                y3 = y2[:].rearrange("p (h l) -> p h l", l=64)
                ins = ve.reduce_sum(gns[:, 0:16], y3, axis=AX)
                vsync(mk, ve, ins)
                mk.wait(ve, "Sg", sg_sq)
                ins = ve.reduce_sum(gns[:, 16:32], ysq[:].rearrange("p (h l) -> p h l", l=64),
                                    axis=AX)
                vsync(mk, ve, ins)
                ins = ve.tensor_scalar_mul(gns[:, 32:48], gns[:, 0:16], 1.0 / 64)
                vsync(mk, ve, ins)
                ins = ve.tensor_mul(gns[:, 80:96], gns[:, 32:48], gns[:, 32:48])
                ve.tensor_scalar(gns[:, 48:64], gns[:, 16:32], 1.0 / 64, EPS_GN,
                                 ALU.mult, ALU.add)
                vsync(mk, ve, ins)
                ins = ve.tensor_sub(gns[:, 48:64], gns[:, 48:64], gns[:, 80:96])
                vsync(mk, ve, ins)
                rsqrt_newton(mk, ve, gns[:, 64:80], gns[:, 48:64], gns[:, 80:96])
                ins = ve.engine_nop()
                vc_rs = mk.inc(ins, "Vc")
                mk.wait(ve, "Vc", vc_rs)
                mu_b = gns[:, 32:48].rearrange("p (h o) -> p h o", o=1).broadcast_to([128, 16, 64])
                rs_b = gns[:, 64:80].rearrange("p (h o) -> p h o", o=1).broadcast_to([128, 16, 64])
                ve.tensor_tensor(y3, y3, mu_b, ALU.subtract)
                ve.tensor_tensor(y3, y3, rs_b, ALU.mult)
                ve.tensor_mul(y2[:], y2[:], gbc[:])
                ve.tensor_add(y2[:], y2[:], bbc[:])
                ins = ve.tensor_mul(gA[cc // 2][:], y2[:], gA[cc // 2][:])
                vc_yg = mk.inc(ins, "Vc")
                if dbg is not None and sci == 0 and cc == 1:
                    mk.wait(gp, "Vc", vc_yg)
                    ins = gp.dma_start(dbg["y2"], y2[:])
                    od_y = mk.inc(ins, "Od", 16)
                    mk.wait(sc, "Od", od_y)
                mk.wait(te, "Vc", vc_yg)
                mk.wait(te, "Sg", mk.ctr["Sg"])
                for mb in range(NKB):
                    ins = te.transpose(PJ[1].bitcast(BF16)[:, mb * 128:(mb + 1) * 128],
                                       gA[cc // 2][:, mb * 128:(mb + 1) * 128], id16[:])
                tm_ygt = mk.inc(ins, "Tm")
                mk.wait(sc, "Tm", tm_ygt)
                tc2 = slice((cc // 2) * 128, (cc // 2) * 128 + 128)
                for mb in range(NKB):
                    ins = sc.copy(ygT16[mb][:, tc2], PJ[1].bitcast(BF16)[:, mb * 128:(mb + 1) * 128])
                mk.inc(ins, "Sg")

        if STAGE < 7:
            continue
        sg_ygt = mk.ctr["Sg"]
        mk.wait(te, "Sg", sg_ygt)
        for mb in range(NKB):
            j = mb % 2
            mk.wait(sy, "Tm", mk.ctr["Tm"] if mb < 2 else tm_wp[j])
            wv_ = stream_wcol(4, mb, j)
            pj = PJ[1][:, 0:SC] if False else (PJ[mb % 2][:, 0:SC])
            mk.wait(te, "Wd", wv_)
            mk.wait(te, "Sg", mk.ctr["Sg"])
            mk.wait(te, "Vc", mk.ctr["Vc"])
            for kb in range(NKB):
                ins = te.matmul(pj, wcb[j][:, kb * 128:(kb + 1) * 128], ygT16[kb][:],
                                start=(kb == 0), stop=(kb == NKB - 1))
            tm_wp[j] = mk.inc(ins, "Tm")
            mk.wait(sc, "Tm", tm_wp[j])
            mk.wait(sc, "Od", mk.ctr["Od"])
            ins = sc.copy(rbuf[mb][:], pj)
            mk.inc(ins, "Sg")
        sg_wo = mk.ctr["Sg"]
        mk.wait(ve, "Sg", sg_wo)
        for mb in range(NKB):
            ins = ve.tensor_add(rbuf[mb][:], rbuf[mb][:], x0T[mb][:, ts4])
        mk.inc(ins, "Vc")
        vc_apply = build_ln(mk, ln, [r[:] for r in rbuf], [x[:, ts4] for x in x0T])
        mk.wait(gp, "Vc", vc_apply)
        for i in range(NKB):
            ins = gp.dma_start(x1_dram[i * 128:(i + 1) * 128, ts4], x0T[i][:, ts4])
            mk.inc(ins, "Od", 16)
    mk.wait(sy, "Od", mk.ctr["Od"])

def host_l0_tables():
    i64 = np.eye(64, dtype=np.float32)
    i64t = np.concatenate([i64] * 16, axis=1).astype(bfdt)
    u1 = np.triu(np.ones((64, 64), np.float32), 1)
    ust = np.concatenate([u1] * 8, axis=1).astype(bfdt)
    rmask = np.ones((128, 256), np.float32)
    rmask[:, 0::64] = 0.0
    hs = np.zeros((128, 128), np.float32)
    for kb in range(8):
        hs[0:64, kb * 16 + 2 * kb] = 1.0
        hs[64:128, kb * 16 + 2 * kb + 1] = 1.0
    return dict(i64t=i64t, ust=ust, rmask=rmask, hsel=hs.astype(bfdt),
                idf=np.eye(128, dtype=np.float32), id16=np.eye(128, dtype=bfdt))






def _mk_fused():
    mk = MK()
    import contextlib as _ctx
    semstack = _ctx.ExitStack()
    mk.stack = semstack
    for s in ["Wd", "Xd", "Wc", "Vc", "Sg", "Tm", "Gc", "Ad", "Od", "Rd"]:
        mk.sem(s)
    nc = mk.nc
    engs = (nc.tensor, nc.vector, nc.scalar, nc.gpsimd, nc.sync)

    x0_d = mk.din("x0", [D, T])
    x1_d = mk.dtmp("x1s", [D, T])
    x2_d = mk.dtmp("x2s", [D, T])
    x3_d = mk.dtmp("x3s", [D, T])
    out_d = mk.dout("out", [D, T])

    def din_px(prefix, shapes, bf16keys):
        d = {}
        for k, s in shapes.items():
            d[k] = mk.din(prefix + k, list(s), BF16 if k in bf16keys else F32)
        return d

    wd_l0 = din_px("Zl0_", dict(maa_x=(128, 8), maa_w=(128, 8), maa_k=(128, 8),
                               maa_v=(128, 8), maa_r=(128, 8), maa_g=(128, 8),
                               tdc=(128, 8), uc=(128, 8), gamr=(1, 1024), betr=(1, 1024),
                               maaw1=(1024, 160), maaw2=(5, 32, 1024), dw1=(1024, 64),
                               dw2=(64, 1024), W_r=(1024, 1024), W_k=(1024, 1024),
                               W_v=(1024, 1024), W_g=(1024, 1024), W_o=(1024, 1024),
                               hsel=(128, 128), i64t=(64, 1024), ust=(64, 512),
                               rmask=(128, 256), idf=(128, 128), id16=(128, 128)),
                   {"hsel", "i64t", "ust", "id16"})
    wd_ffn = din_px("Zfn_", dict(w1=(1024, 4096), b1c=(128, 32), w2=(4096, 1024),
                                b2c=(128, 8)), set())
    wd_at = din_px("Zat_", dict(wq=(1024, 1024), wk=(1024, 1024), wv=(1024, 1024),
                               wo=(1024, 1024), ropeC=(128, T), ropeS=(128, T),
                               pswap=(128, 128), hsel=(128, 128), dmask=(128, 2048)),
                   {"ropeC", "ropeS", "pswap", "hsel", "dmask"})
    wd_mo = din_px("Zmo_", dict(router=(1024, 8), gate=(8, 1024, 1024), up=(8, 1024, 1024),
                               bgc=(128, 64), buc=(128, 64), down=(8, 1024, 1024),
                               dbias=(8, 1024), idf=(128, 128)), set())

    for builder, args in [(build_l0_phase, (x0_d, x1_d, wd_l0)),
                          (build_ffn_phase, (x1_d, x2_d, wd_ffn)),
                          (build_attn_phase, (x2_d, x3_d, wd_at)),
                          (build_moe_phase, (x3_d, out_d, wd_mo))]:
        mk.stack = _ctx.ExitStack()
        builder(mk, *args)
        mk.stack.close()
        mk.join(*engs)
    mk.stack = semstack
    return mk


def kernel(**inputs):
    z = inputs
    NCORES = 8
    l0w = dict(
        maa_x=host_col(z["L0_attn_time_maa_x"]), maa_w=host_col(z["L0_attn_time_maa_w"]),
        maa_k=host_col(z["L0_attn_time_maa_k"]), maa_v=host_col(z["L0_attn_time_maa_v"]),
        maa_r=host_col(z["L0_attn_time_maa_r"]), maa_g=host_col(z["L0_attn_time_maa_g"]),
        tdc=host_col(z["L0_attn_time_decay"]),
        uc=host_col(np.asarray(z["L0_attn_time_faaaa"]).reshape(-1)),
        gamr=np.ascontiguousarray(np.asarray(z["L0_attn_ln_gamma"], np.float32).reshape(1, -1)),
        betr=np.ascontiguousarray(np.asarray(z["L0_attn_ln_beta"], np.float32).reshape(1, -1)),
        maaw1=np.ascontiguousarray(z["L0_attn_time_maa_w1"], dtype=np.float32),
        maaw2=np.ascontiguousarray(z["L0_attn_time_maa_w2"], dtype=np.float32),
        dw1=np.ascontiguousarray(z["L0_attn_time_decay_w1"], dtype=np.float32),
        dw2=np.ascontiguousarray(z["L0_attn_time_decay_w2"], dtype=np.float32),
        W_r=np.ascontiguousarray(z["L0_attn_W_r"], dtype=np.float32),
        W_k=np.ascontiguousarray(z["L0_attn_W_k"], dtype=np.float32),
        W_v=np.ascontiguousarray(z["L0_attn_W_v"], dtype=np.float32),
        W_g=np.ascontiguousarray(z["L0_attn_W_g"], dtype=np.float32),
        W_o=np.ascontiguousarray(z["L0_attn_W_o"], dtype=np.float32),
    )
    l0w.update(host_l0_tables())
    ffnw = dict(w1=np.ascontiguousarray(z["L0_ffn_w1"], dtype=np.float32),
                b1c=host_col(np.asarray(z["L0_ffn_b1"])),
                w2=np.ascontiguousarray(z["L0_ffn_w2"], dtype=np.float32),
                b2c=host_col(np.asarray(z["L0_ffn_b2"])))
    atw = host_attn_weights(np.asarray(z["L1_attn_W_qkv"], np.float32),
                            np.asarray(z["L1_attn_W_o"], np.float32))
    atw.update(host_attn_tables())
    mow = host_moe_weights(np.asarray(z["L1_ffn_router_weight"], np.float32),
                           np.asarray(z["L1_ffn_gate_up_proj"], np.float32),
                           np.asarray(z["L1_ffn_gate_up_proj_bias"], np.float32),
                           np.asarray(z["L1_ffn_down_proj"], np.float32),
                           np.asarray(z["L1_ffn_down_proj_bias"], np.float32))
    x0 = np.asarray(z["L0_attn_x"], np.float32)

    allw = {}
    for pre, d in [("Zl0_", l0w), ("Zfn_", ffnw), ("Zat_", atw), ("Zmo_", mow)]:
        for k, v in d.items():
            allw[pre + k] = v
    mk = _mk_fused()
    in_maps = [dict(allw, x0=np.ascontiguousarray(x0[b].T)) for b in range(NCORES)]
    res = run_bass_kernel_spmd(mk.nc, in_maps, core_ids=list(range(NCORES)))
    out = np.stack([np.ascontiguousarray(res.results[b]["out"].T) for b in range(NCORES)])
    return out
